# version 02
# baseline (speedup 1.0000x reference)
import numpy as np
from contextlib import ExitStack

import concourse.bass as bass
import concourse.mybir as mybir
from concourse.bass_utils import run_bass_kernel_spmd

F32 = mybir.dt.float32
BF16 = mybir.dt.bfloat16
AF = mybir.ActivationFunctionType
OP = mybir.AluOpType
AX = mybir.AxisListType

B, L, D = 4, 2048, 2048
T = 1024
FT = 16
H = 32
D_SSM = 2048
CONV = 2560
PROJ = 4640
CT = 20
CSTR = 1028
Q = 128
NCH = 8
NH = 16
E_MOE, I_MOE, IT_MOE = 8, 1408, 11
SH_I = 2816
EPS = 1e-5
RSQ = float(1.0 / np.sqrt(np.float32(128.0)))

NSUB = 4
SCAN_ONLY = False
STAGE = 99
SCANPART = 99
MOESTAGE = 99
PAIR_GROUPS = [[0, 1], [2, 3], [4, 5], [6, 7]]


class Clock:
    __slots__ = ("sem", "n", "name")

    def __init__(self, sem, name):
        self.sem, self.n, self.name = sem, 0, name


class KB:
    def __init__(self, nc, es):
        self.nc, self.es = nc, es
        self.clks = {}
        self.cur = None
        self.e = None
        self.marks = {}

    def clk(self, name):
        sem = self.es.enter_context(self.nc.semaphore(name))
        c = Clock(sem, name)
        self.clks[name] = c
        return c

    def begin(self, engine_name, handle):
        self.cur, self.e = engine_name, handle
        for c in self.clks.values():
            c.n = 0
        self.marks = {}

    def wait(self, eng, clock, val=None):
        v = clock.n if val is None else val
        if v is None or v <= 0:
            return
        key = (eng, clock.name)
        if self.marks.get(key, 0) >= v:
            return
        self.marks[key] = v
        if self.cur == eng:
            self.e.wait_ge(clock.sem, v)

    def op(self, eng, fn, inc=None, inc_by=1, waits=()):
        for wc in waits:
            if isinstance(wc, tuple):
                self.wait(eng, wc[0], wc[1])
            elif wc is not None:
                self.wait(eng, wc)
        tok = None
        if inc is not None:
            inc.n += inc_by
            tok = inc.n
        if self.cur == eng:
            inst = fn(self.e)
            if inc is not None:
                inst.then_inc(inc.sem, inc_by)
        return tok


def build_program():
    nc = bass.Bass("TRN2", target_bir_lowering=False, debug=False)
    es = ExitStack()
    kb = KB(nc, es)

    def din(name, shape):
        return nc.dram_tensor(name, shape, F32, kind="ExternalInput").ap()

    x_fm = din("x_fm", [D, T])
    xbnd_t = din("xbnd_t", [128, 64])
    bflag = din("bflag", [128, 1])
    bsel = din("bsel", [128, 4])
    bin_h = din("bin_h", [128, 37])
    mup_h = din("mup_h", [128, 37])
    convw_h = din("convw_h", [128, 80])
    convb_h = din("convb_h", [128, 20])
    normw_h = din("normw_h", [128, 16])
    bout_h = din("bout_h", [128, 16])
    bprj_h = din("bprj_h", [128, 16])
    a32_h = din("a32_h", [32, 3])
    gwt_h = din("gwt_h", [128, 128])
    W_in = din("W_in", [D, PROJ])
    b_in = din("b_in", [PROJ])
    mup_vec = din("mup_vec", [PROJ])
    conv_w = din("conv_w", [CONV, 4])
    conv_b = din("conv_b", [CONV])
    A_log = din("A_log", [H])
    D_par = din("D_par", [H])
    dt_bias = din("dt_bias", [H])
    norm_w = din("norm_w", [D_SSM])
    W_out = din("W_out", [D_SSM, D])
    b_out = din("b_out", [D])
    WqkvT = din("WqkvT", [D, 3 * D])
    WprojT = din("WprojT", [D, D])
    bproj = din("bproj", [D])
    gate_wT = din("gate_wT", [D, E_MOE])
    w1 = din("w1", [E_MOE, D, I_MOE])
    w3 = din("w3", [E_MOE, D, I_MOE])
    w2 = din("w2", [E_MOE, I_MOE, D])
    sw1 = din("sw1", [D, SH_I])
    sw3 = din("sw3", [D, SH_I])
    sw2 = din("sw2", [SH_I, D])

    out_fm = nc.dram_tensor("out_fm", [D, T], F32, kind="ExternalOutput").ap()

    st_src = nc.dram_tensor("st_src", [128, 512], BF16)
    st_dst = nc.dram_tensor("st_dst", [2, 128, 512], BF16)
    qdram = nc.dram_tensor("qdram", [NH, 128, T], BF16)
    ctx_src = nc.dram_tensor("ctx_src", [NH, 128, T], BF16)
    ctx_dst = nc.dram_tensor("ctx_dst", [4, NH, 128, T], BF16)
    kv_src = nc.dram_tensor("kv_src", [NH, 2, 128, T], BF16)
    kv_dst = nc.dram_tensor("kv_dst", [2, NH, 2, 128, T], BF16)

    SBUF = lambda name, shape, dt: es.enter_context(nc.sbuf_tensor(name, shape, dt))
    PSUM = lambda name, shape, dt: es.enter_context(nc.psum_tensor(name, shape, dt))

    RES = SBUF("RES", [128, FT * T], BF16)
    UB = SBUF("UB", [128, FT * T], BF16)
    YTM = SBUF("YTM", [128, FT * T], BF16)
    XBC = SBUF("XBC", [128, CT * CSTR], BF16)
    WS = SBUF("WS", [128, 13824], BF16)
    LNB = SBUF("LNB", [128, 2 * T], BF16)
    MF = SBUF("MF", [128, 2560], F32)
    MB = SBUF("MB", [128, 5888], BF16)

    bin_t = SBUF("bin_t", [128, 37], F32)
    mup_t = SBUF("mup_t", [128, 37], F32)
    bmup = SBUF("bmup", [128, 37], F32)
    convw_t = SBUF("convw_t", [128, 80], F32)
    convb_t = SBUF("convb_t", [128, 20], F32)
    normw_t = SBUF("normw_t", [128, 16], F32)
    bout_t = SBUF("bout_t", [128, 16], F32)
    bprj_t = SBUF("bprj_t", [128, 16], F32)
    a32 = SBUF("a32", [32, 4], F32)
    dtb2 = SBUF("dtb2", [32, 1], F32)
    flag_t = SBUF("flag_t", [128, 1], F32)
    bsel_t = SBUF("bsel_t", [128, 4], F32)
    dbc = SBUF("dbc", [128, 32], F32)
    row32 = SBUF("row32", [1, 32], F32)
    gwt = SBUF("gwt", [128, FT * E_MOE], BF16)
    UBND = SBUF("UBND", [128, 48], BF16)
    XB3 = SBUF("XB3", [128, 64], BF16)
    SQ3 = SBUF("SQ3", [128, 64], BF16)
    LNB3 = SBUF("LNB3", [128, 8], F32)
    STAT3 = SBUF("STAT3", [1, 32], F32)

    ident = SBUF("ident", [128, 128], BF16)
    onesc = SBUF("onesc", [128, 1], BF16)
    onesrb = SBUF("onesrb", [1, 128], BF16)
    onesf = SBUF("onesf", [128, 128], F32)
    identf = SBUF("identf", [128, 128], F32)
    mask = SBUF("mask", [128, 128], F32)

    SST = SBUF("SST", [128, 512], BF16)
    SSIB = SBUF("SSIB", [128, 512], BF16)
    SROWS = SBUF("SROWS", [32, 2048], F32)
    SCUM = SROWS[:, 0:1024]
    DTT = SROWS[:, 1024:2048]
    EDT = MF[0:32, 1024:2048]
    LA = MF[0:32, 1024:2048]
    EIN = MF[0:32, 1024:2048]
    GCUM = MF[0:32, 0:1024]
    EG = MF[0:32, 0:1024]
    LNBF = LNB[:].bitcast(F32)
    DT_TM = LNBF[:, 0:256]
    S_TM = LNBF[:, 256:512]
    EIN_TM = LNBF[:, 512:768]
    EG_TM = LNBF[:, 768:1024]
    GAM = SBUF("GAM", [128, 256], F32)
    EPSC = SBUF("EPSC", [128, 1], F32)
    MASKB = SBUF("MASKB", [128, 128], BF16)
    NEA_B = SBUF("NEA_B", [128, 32], F32)
    PEA_B = SBUF("PEA_B", [128, 32], F32)
    rowsc = SBUF("rowsc", [1, 32], F32)
    SELG = MB[0:8, 3072:4096]
    RSTDB = MB[0:8, 2048:3072]
    STATR = MF[0:1, 0:2048]
    VARB = MF[0:8, 0:1024]
    RSTD = MF[0:8, 1024:2048]
    GON = SBUF("GON", [128, FT * 8], BF16)
    COEF = SBUF("COEF", [128, 64], F32)
    LGTM = SBUF("LGTM", [128, 64], F32)
    GSC = SBUF("GSC", [128, 96], F32)
    LROW = MF[0:1, 1024:2048]
    LINV = MF[:, 0:1024]
    CC = SBUF("CC", [128, 1], F32)
    MXB = SBUF("MXB", [128, 2], BF16)
    HSB = XBC[:, 12340: 12340 + 2048]
    BOU = XBC[:, 12340: 12340 + 2048]
    QH = XBC[:, 14388: 14388 + 2048]
    KH = XBC[:, 0: 4096]
    VH = XBC[:, 4096: 8192]
    VTM = XBC[:, 8192: 10240]

    PA = PSUM("PA", [128, 512], F32)
    PB = PSUM("PB", [128, 512], F32)
    PC = PSUM("PC", [128, 512], F32)
    PD = PSUM("PD", [128, 512], F32)
    PE1 = PSUM("PE1", [128, 512], F32)
    PF = PSUM("PF", [128, 512], F32)
    PT1 = PSUM("PT1", [128, 1024], BF16)
    PT2 = PSUM("PT2", [128, 1024], BF16)
    PT1f = PT1[:].bitcast(F32)
    PT2f = PT2[:].bitcast(F32)
    MF_b = MF[:].bitcast(BF16)

    nc.const_aps.register(F32, EPS, EPSC[0:1, 0:1])

    ck_t = kb.clk("ck_t")
    ck_v = kb.clk("ck_v")
    ck_s = kb.clk("ck_s")
    ck_g = kb.clk("ck_g")
    wd_clks = [kb.clk(f"ck_wd{i}") for i in range(4)]
    ad_clks = [kb.clk(f"ck_ad{i}") for i in range(4)]
    ck_cc = kb.clk("ck_cc")
    ROT = {"wd": 0, "ad": 0}

    TEN, VEC, SCA, GPS, SYN = "tensor", "vector", "scalar", "gpsimd", "sync"
    ALL_CLKS = (ck_t, ck_v, ck_s, ck_g, ck_cc) + tuple(wd_clks) + tuple(ad_clks)

    def barrier():
        for eng in (TEN, VEC, SCA, GPS, SYN):
            for c in ALL_CLKS:
                kb.wait(eng, c)

    def mm(ps, lhsT, rhs, start, stop, inc=None, waits=(), tp=None):
        return kb.op(TEN, lambda e: e.matmul(ps, lhsT, rhs, start=start, stop=stop,
                                             tile_position=tp),
                     inc=inc, waits=waits)

    def transp(ps, in_ap, inc=None, waits=()):
        pdim = in_ap.shape[0]
        idt = identf if in_ap.dtype == F32 else ident
        return kb.op(TEN, lambda e: e.transpose(ps, in_ap, idt[0:pdim, 0:pdim]),
                     inc=inc, waits=waits)

    def vop(fn, waits=()):
        kb.wait(VEC, ck_v, ck_v.n)
        return kb.op(VEC, fn, inc=ck_v, waits=waits)

    def sop(fn, waits=()):
        kb.wait(SCA, ck_s, ck_s.n)
        return kb.op(SCA, fn, inc=ck_s, waits=waits)

    def gop(fn, waits=()):
        kb.wait(GPS, ck_g, ck_g.n)
        return kb.op(GPS, fn, inc=ck_g, waits=waits)

    def wdma(out_ap, in_ap, waits=(), slow=False):
        kb.wait(GPS, ck_g, ck_g.n)
        c = wd_clks[ROT["wd"] % 4]
        ROT["wd"] += 1
        kb.wait(GPS, c, c.n)
        v = kb.op(GPS, lambda e: e.dma_start(out=out_ap, in_=in_ap,
                                             allow_slow_non_contiguous=slow),
                  inc=c, inc_by=16, waits=waits)
        return (c, v)

    def adma(out_ap, in_ap, waits=()):
        c = ad_clks[ROT["ad"] % 4]
        ROT["ad"] += 1
        kb.wait(SYN, c, c.n)
        v = kb.op(SYN, lambda e: e.dma_start(out=out_ap, in_=in_ap),
                  inc=c, inc_by=16, waits=waits)
        return (c, v)

    def wait_all_dma(eng):
        for c in wd_clks + ad_clks:
            kb.wait(eng, c)

    def setup():
        wdma(RES[:].rearrange("p (f t) -> p f t", f=FT),
             x_fm.rearrange("(f p) t -> p f t", p=128))
        wdma(XB3[:], xbnd_t[:, :])
        wdma(bin_t[:], bin_h[:, :])
        wdma(mup_t[:], mup_h[:, :])
        wdma(convw_t[:], convw_h[:, :])
        wdma(convb_t[:], convb_h[:, :])
        wdma(normw_t[:], normw_h[:, :])
        wdma(bout_t[:], bout_h[:, :])
        wdma(bprj_t[:], bprj_h[:, :])
        wdma(a32[:, 0:3], a32_h[:, :])
        wdma(flag_t[:], bflag[:, :])
        wdma(bsel_t[:], bsel[:, :])
        wdma(row32[:], D_par.rearrange("(o p) -> o p", o=1), slow=True)
        wdma(gwt[:], gwt_h[:, :])
        t_small = [(c, c.n) for c in wd_clks]

        gop(lambda e: e.memset(EPSC[:], EPS))
        gop(lambda e: e.memset(ident[:], 0.0))
        gop(lambda e: e.memset(mask[:], 0.0))
        gop(lambda e: e.memset(onesc[:], 1.0))
        gop(lambda e: e.memset(onesrb[:], 1.0))
        gop(lambda e: e.memset(onesf[:], 1.0))
        gop(lambda e: e.memset(GON[:], 0.0))
        gop(lambda e: e.memset(SST[:], 0.0))
        gop(lambda e: e.affine_select(out=ident[:], in_=ident[:],
                                      compare_op=OP.not_equal, fill=1.0, base=0,
                                      pattern=[[-1, 128]], channel_multiplier=1))
        gop(lambda e: e.memset(identf[:], 0.0))
        gop(lambda e: e.affine_select(out=identf[:], in_=identf[:],
                                      compare_op=OP.not_equal, fill=1.0, base=0,
                                      pattern=[[-1, 128]], channel_multiplier=1))
        gop(lambda e: e.affine_select(out=mask[:], in_=mask[:],
                                      compare_op=OP.is_gt, fill=1.0, base=0,
                                      pattern=[[-1, 128]], channel_multiplier=1))
        gop(lambda e: e.memset(MASKB[:], 0.0))
        gop(lambda e: e.affine_select(out=MASKB[:], in_=MASKB[:],
                                      compare_op=OP.is_gt, fill=1.0, base=0,
                                      pattern=[[-1, 128]], channel_multiplier=1))

        for o in range(FT):
            gop(lambda e, o=o: e.memset(GON[:, o * 8 + o // 2: o * 8 + o // 2 + 1], 1.0))
        for o in range(CT):
            gop(lambda e, o=o: e.memset(XBC[:, o * CSTR: o * CSTR + 4], 0.0))

        vop(lambda e: e.tensor_mul(bmup[:], bin_t[:], mup_t[:]),
            waits=t_small)
        sop(lambda e: e.activation(a32[:, 3:4], a32[:, 0:1], AF.Exp),
            waits=t_small)
        sop(lambda e: e.mul(a32[:, 3:4], a32[:, 3:4], -1.0))
        vop(lambda e: e.tensor_add(dtb2[:], bmup[0:32, 36:37], a32[0:32, 2:3]),
            waits=[(ck_s, ck_s.n)])
        mm(PF[:, 0:32], onesf[0:1, 0:128], row32[:], True, True, inc=ck_t,
           waits=t_small + [(ck_g, ck_g.n)])
        vop(lambda e: e.tensor_copy(dbc[:], PF[:, 0:32]), waits=[(ck_t, ck_t.n)])
        s_nea = ck_s.n
        tnea = transp(PT1f[0:1, 0:32], a32[0:32, 3:4], inc=ck_t, waits=[(ck_s, s_nea)])
        vop(lambda e: e.tensor_copy(rowsc[:], PT1f[0:1, 0:32]), waits=[(ck_t, tnea)])
        v_row = ck_v.n
        mm(PE1[:, 0:32], onesf[0:1, 0:128], rowsc[:], True, True, inc=ck_t,
           waits=[(ck_v, v_row)])
        vop(lambda e: e.tensor_copy(NEA_B[:], PE1[:, 0:32]), waits=[(ck_t, ck_t.n)])
        vop(lambda e: e.tensor_scalar(PEA_B[:], NEA_B[:], -1.0, None, OP.mult))
        barrier()

    def layernorm():
        barrier()
        for f in range(FT):
            for th in range(2):
                bank = PA if th == 0 else PB
                mm(bank[0:1, 0:512], onesc[:],
                   RES[:, f * T + th * 512: f * T + th * 512 + 512],
                   start=(f == 0), stop=(f == FT - 1),
                   inc=ck_t if (f == FT - 1 and th == 1) else None)
        t_sx = ck_t.n
        sq_tick, mmt = {}, {}
        for f in range(FT):
            w = [(ck_t, mmt.get(f - 2))] if f >= 2 else []
            sq_tick[f] = sop(lambda e, f=f: e.activation(
                MB[:, (f % 2) * 1024: (f % 2) * 1024 + 1024],
                RES[:, f * T: f * T + T], AF.Square), waits=w)
            for th in range(2):
                bank = PC if th == 0 else PD
                mmt[f] = mm(bank[0:1, 0:512], onesc[:],
                            MB[:, (f % 2) * 1024 + th * 512: (f % 2) * 1024 + th * 512 + 512],
                            start=(f == 0), stop=(f == FT - 1),
                            inc=ck_t, waits=[(ck_s, sq_tick[f])])
        t_sx2 = ck_t.n
        vop(lambda e: e.tensor_copy(STATR[0:1, 0:512], PA[0:1, 0:512]), waits=[(ck_t, t_sx)])
        vop(lambda e: e.tensor_copy(STATR[0:1, 512:1024], PB[0:1, 0:512]))
        vop(lambda e: e.tensor_copy(STATR[0:1, 1024:1536], PC[0:1, 0:512]),
            waits=[(ck_t, t_sx2)])
        vop(lambda e: e.tensor_copy(STATR[0:1, 1536:2048], PD[0:1, 0:512]))
        vop(lambda e: e.tensor_scalar(STATR[0:1, 0:2048], STATR[0:1, 0:2048],
                                      1.0 / D, None, OP.mult))
        for hh2 in range(2):
            vop(lambda e, hh2=hh2: e.tensor_mul(
                MF[0:1, 2048: 2048 + 512],
                STATR[0:1, hh2 * 512: hh2 * 512 + 512],
                STATR[0:1, hh2 * 512: hh2 * 512 + 512]))
            vop(lambda e, hh2=hh2: e.tensor_sub(
                STATR[0:1, 1024 + hh2 * 512: 1024 + hh2 * 512 + 512],
                STATR[0:1, 1024 + hh2 * 512: 1024 + hh2 * 512 + 512],
                MF[0:1, 2048: 2048 + 512]))
        v2 = ck_v.n
        sop(lambda e: e.activation(STATR[0:1, 1024:2048], STATR[0:1, 1024:2048],
                                   AF.Sqrt, bias=EPS), waits=[(ck_v, v2)])
        s1 = ck_s.n
        vop(lambda e: e.reciprocal(STATR[0:1, 1024:2048], STATR[0:1, 1024:2048]),
            waits=[(ck_s, s1)])
        vop(lambda e: e.tensor_mul(STATR[0:1, 0:1024], STATR[0:1, 0:1024],
                                   STATR[0:1, 1024:2048]))
        vop(lambda e: e.tensor_scalar(STATR[0:1, 0:1024], STATR[0:1, 0:1024],
                                      -1.0, None, OP.mult))
        v4 = ck_v.n
        ev_a = ev_b = None
        for th in range(2):
            mm(PE1[:, 0:512], onesf[0:1, 0:128],
               STATR[0:1, 1024 + th * 512: 1024 + th * 512 + 512], True, True,
               inc=ck_t, waits=[(ck_v, v4), (ck_v, ev_a)])
            ev_a = vop(lambda e, th=th: e.tensor_copy(LNB[:, th * 512: th * 512 + 512],
                                                      PE1[:, 0:512]), waits=[(ck_t, ck_t.n)])
            mm(PF[:, 0:512], onesf[0:1, 0:128],
               STATR[0:1, th * 512: th * 512 + 512], True, True,
               inc=ck_t, waits=[(ck_v, v4), (ck_v, ev_b)])
            ev_b = vop(lambda e, th=th: e.tensor_copy(LNB[:, 1024 + th * 512: 1024 + th * 512 + 512],
                                                      PF[:, 0:512]), waits=[(ck_t, ck_t.n)])
        v_b = ck_v.n
        for f in range(FT):
            vop(lambda e, f=f: e.tensor_mul(
                MB[:, 2048 + (f % 2) * 1024: 2048 + (f % 2) * 1024 + 1024],
                RES[:, f * T: f * T + T], LNB[:, 0:1024]), waits=[(ck_v, v_b)])
            vop(lambda e, f=f: e.tensor_add(
                UB[:, f * T: f * T + T],
                MB[:, 2048 + (f % 2) * 1024: 2048 + (f % 2) * 1024 + 1024],
                LNB[:, 1024:2048]))
        barrier()

    def mamba_boundary_ln():
        sop(lambda e: e.activation(SQ3[:], XB3[:], AF.Square))
        s_sq3 = ck_s.n
        for f in range(FT):
            mm(PE1[0:1, 0:4], onesc[:], XB3[:, f * 4: f * 4 + 4],
               start=(f == 0), stop=(f == FT - 1),
               inc=ck_t if f == FT - 1 else None)
        for f in range(FT):
            mm(PF[0:1, 0:4], onesc[:], SQ3[:, f * 4: f * 4 + 4],
               start=(f == 0), stop=(f == FT - 1),
               inc=ck_t if f == FT - 1 else None, waits=[(ck_s, s_sq3)])
        t_b = ck_t.n
        vop(lambda e: e.tensor_copy(STAT3[0:1, 0:4], PE1[0:1, 0:4]), waits=[(ck_t, t_b)])
        vop(lambda e: e.tensor_copy(STAT3[0:1, 4:8], PF[0:1, 0:4]))
        vop(lambda e: e.tensor_scalar(STAT3[0:1, 8:16], STAT3[0:1, 0:8],
                                      1.0 / D, None, OP.mult))
        vop(lambda e: e.tensor_mul(STAT3[0:1, 16:20], STAT3[0:1, 8:12], STAT3[0:1, 8:12]))
        vop(lambda e: e.tensor_sub(STAT3[0:1, 12:16], STAT3[0:1, 12:16], STAT3[0:1, 16:20]))
        v2 = ck_v.n
        sop(lambda e: e.activation(STAT3[0:1, 12:16], STAT3[0:1, 12:16], AF.Sqrt,
                                   bias=EPS), waits=[(ck_v, v2)])
        s2 = ck_s.n
        vop(lambda e: e.reciprocal(STAT3[0:1, 12:16], STAT3[0:1, 12:16]),
            waits=[(ck_s, s2)])
        vop(lambda e: e.tensor_mul(STAT3[0:1, 8:12], STAT3[0:1, 8:12], STAT3[0:1, 12:16]))
        vop(lambda e: e.tensor_scalar(STAT3[0:1, 8:12], STAT3[0:1, 8:12],
                                      -1.0, None, OP.mult))
        v4 = ck_v.n
        mm(PE1[:, 0:4], onesf[0:1, 0:128], STAT3[0:1, 12:16], True, True, inc=ck_t,
           waits=[(ck_v, v4)])
        vop(lambda e: e.tensor_copy(LNB3[:, 0:4], PE1[:, 0:4]), waits=[(ck_t, ck_t.n)])
        mm(PF[:, 0:4], onesf[0:1, 0:128], STAT3[0:1, 8:12], True, True, inc=ck_t,
           waits=[(ck_v, ck_v.n)])
        vop(lambda e: e.tensor_copy(LNB3[:, 4:8], PF[:, 0:4]), waits=[(ck_t, ck_t.n)])
        for f in range(FT):
            vop(lambda e, f=f: e.tensor_mul(MF[:, 1024 + f * 3: 1024 + f * 3 + 3],
                                            XB3[:, f * 4: f * 4 + 3], LNB3[:, 0:3]))
            vop(lambda e, f=f: e.tensor_add(UBND[:, f * 3: f * 3 + 3],
                                            MF[:, 1024 + f * 3: 1024 + f * 3 + 3],
                                            LNB3[:, 4:7]))
        return ck_v.n

    def mamba_inproj(v_ubnd):
        banks = (PA, PB, PC, PD)
        blk = FT * 128
        mm_done, epi_done, bnd_s, bnd_v = {}, {}, {}, {}
        for oi in range(37):
            ow = 128 if oi < 36 else 32
            bsl = WS[:, (oi % 2) * blk: (oi % 2) * blk + FT * ow]
            w = [(ck_t, mm_done.get(oi - 2))] if oi >= 2 else []
            dtok = wdma(bsl.rearrange("p (kt o) -> p kt o", kt=FT),
                        W_in[:, oi * 128: oi * 128 + ow]
                        .rearrange("(kt p) o -> p kt o", p=128), waits=w)
            tk = None
            for th in range(2):
                bank = banks[(oi * 2 + th) % 4]
                kb.wait(TEN, dtok[0], dtok[1])
                if oi * 2 + th >= 4:
                    kb.wait(TEN, ck_s, epi_done.get(oi * 2 + th - 4))
                for kt in range(FT):
                    tk = mm(bank[0:ow, 0:512], bsl[:, kt * ow: kt * ow + ow],
                            UB[:, kt * T + th * 512: kt * T + th * 512 + 512],
                            start=(kt == 0), stop=(kt == FT - 1),
                            inc=ck_t if kt == FT - 1 else None)
                if oi < 16:
                    epi_done[oi * 2 + th] = sop(
                        lambda e, oi=oi, th=th, bank=bank: e.activation(
                            YTM[:, oi * T + th * 512: oi * T + th * 512 + 512],
                            bank[:, 0:512], AF.Silu,
                            bias=bmup[:, oi: oi + 1], scale=mup_t[:, oi: oi + 1]),
                        waits=[(ck_t, tk)])
                elif oi < 36:
                    o = oi - 16
                    epi_done[oi * 2 + th] = sop(
                        lambda e, o=o, oi=oi, th=th, bank=bank: e.activation(
                            XBC[:, o * CSTR + 4 + th * 512: o * CSTR + 4 + th * 512 + 512],
                            bank[:, 0:512], AF.Identity,
                            bias=bmup[:, oi: oi + 1], scale=mup_t[:, oi: oi + 1]),
                        waits=[(ck_t, tk)])
                else:
                    sop(lambda e, th=th, bank=bank: e.activation(
                        EDT[:, th * 512: th * 512 + 512], bank[0:32, 0:512],
                        AF.Exp, bias=dtb2[:, 0:1], scale=mup_t[0:32, 36:37]),
                        waits=[(ck_t, tk)])
                    epi_done[oi * 2 + th] = sop(
                        lambda e, th=th: e.activation(
                            DTT[:, th * 512: th * 512 + 512],
                            EDT[:, th * 512: th * 512 + 512], AF.Ln, bias=1.0))
            mm_done[oi] = tk
            if 16 <= oi < 36:
                o = oi - 16
                kb.wait(TEN, ck_v, v_ubnd)
                kb.wait(TEN, ck_s, bnd_s.get(o - 1))
                tkb = None
                for kt in range(FT):
                    tkb = mm(PE1[:, 0:3], bsl[:, kt * 128: kt * 128 + 128],
                             UBND[:, kt * 3: kt * 3 + 3],
                             start=(kt == 0), stop=(kt == FT - 1),
                             inc=ck_t if kt == FT - 1 else None)
                mm_done[oi] = tkb
                bnd_s[o] = sop(lambda e, o=o, oi=oi: e.activation(
                    MF[:, 2048 + (o % 2) * 4: 2048 + (o % 2) * 4 + 3], PE1[:, 0:3],
                    AF.Identity, bias=bmup[:, oi: oi + 1], scale=mup_t[:, oi: oi + 1]),
                    waits=[(ck_t, tkb), (ck_v, bnd_v.get(o - 2))])
                bnd_v[o] = vop(lambda e, o=o: e.tensor_scalar(
                    XBC[:, o * CSTR + 1: o * CSTR + 4],
                    MF[:, 2048 + (o % 2) * 4: 2048 + (o % 2) * 4 + 3],
                    flag_t[:, 0:1], None, OP.mult), waits=[(ck_s, bnd_s[o])])
        barrier()

    def mamba_conv():
        for o in range(CT):
            base = o * CSTR
            vop(lambda e, base=base, o=o: e.tensor_scalar(
                MF[:, 0:1024], XBC[:, base + 1: base + 1 + 1024],
                convw_t[:, o * 4: o * 4 + 1], None, OP.mult))
            for k in range(1, 4):
                vop(lambda e, base=base, o=o, k=k: e.scalar_tensor_tensor(
                    MF[:, 0:1024], XBC[:, base + 1 + k: base + 1 + k + 1024],
                    convw_t[:, o * 4 + k: o * 4 + k + 1], MF[:, 0:1024],
                    OP.mult, OP.add))
            v_c = ck_v.n
            sconv = sop(lambda e, base=base, o=o: e.activation(
                XBC[:, base + 4: base + 4 + 1024], MF[:, 0:1024], AF.Silu,
                bias=convb_t[:, o: o + 1]), waits=[(ck_v, v_c)])
            kb.wait(VEC, ck_s, sconv)
        barrier()

    def mamba_rows():
        sop(lambda e: e.mul(LA, DTT, a32[:, 3:4]))
        s_la = ck_s.n
        for c in range(NCH):
            vop(lambda e, c=c: e.tensor_tensor_scan(
                SCUM[:, c * Q: c * Q + Q], onesf[0:32, 0:128],
                LA[:, c * Q: c * Q + Q], 0.0, OP.mult, OP.add),
                waits=[(ck_s, s_la)])
        for c in range(NCH):
            init = 0.0 if c == 0 else GCUM[:, c * Q - 1: c * Q]
            vop(lambda e, c=c, init=init: e.tensor_tensor_scan(
                GCUM[:, c * Q: c * Q + Q], onesf[0:32, 0:128],
                LA[:, c * Q: c * Q + Q], init, OP.mult, OP.add))
        v_sc = ck_v.n
        sop(lambda e: e.activation(EIN, SCUM, AF.Exp), waits=[(ck_v, v_sc)])
        sop(lambda e: e.activation(EG, GCUM, AF.Exp))
        s_e = ck_s.n
        vop(lambda e: e.tensor_scalar(EG, EG, flag_t[0:32, 0:1], None, OP.mult),
            waits=[(ck_s, s_e)])
        v_eg = ck_v.n
        tr_ev = {}
        i = 0
        for (srcT, dstT) in ((DTT, DT_TM), (EIN, EIN_TM), (EG, EG_TM)):
            for c in range(NCH):
                pf = PT1f if i % 2 == 0 else PT2f
                w = [(ck_v, v_eg), (ck_s, s_e)]
                if i >= 2:
                    w.append((ck_v, tr_ev[i - 2]))
                tt = transp(pf[0:128, 0:32], srcT[0:32, c * Q: c * Q + Q], inc=ck_t,
                            waits=w)
                tr_ev[i] = vop(lambda e, pf=pf, dstT=dstT, c=c: e.tensor_copy(
                    dstT[:, c * 32: c * 32 + 32], pf[0:128, 0:32]), waits=[(ck_t, tt)])
                i += 1
        barrier()

    XTM0, YTMB0, ATB0, XT0, XD0, BTM0 = 0, 2048, 4096, 5120, 5376, 5632

    def scan_chunk(c, wave_fin, tr_state, g_ev):
        kb.wait(TEN, ck_s, ck_s.n)
        for wv in range(2):
            pt = PT1 if wv == 0 else PT2
            w = [(ck_v, tr_state.get(("pt", wv)))]
            tk = None
            for i in range(8):
                f = wv * 8 + i
                tk = transp(pt[:, i * 128: i * 128 + 128],
                            XBC[:, f * CSTR + 4 + c * Q: f * CSTR + 4 + c * Q + Q],
                            inc=ck_t if i == 7 else None, waits=w if i == 0 else ())
            ev = vop(lambda e, pt=pt, wv=wv: e.tensor_copy(
                MB[:, XTM0 + wv * 1024: XTM0 + wv * 1024 + 1024], pt[:]),
                waits=[(ck_t, tk)])
            tr_state[("pt", wv)] = ev
        v_xtm = ck_v.n
        if SCANPART <= 1:
            return
        tk = None
        for i in range(2):
            w = [(ck_v, tr_state[("pt", 0)])] if i == 0 else ()
            tk = transp(PT1[:, i * 128: i * 128 + 128],
                        XBC[:, (16 + i) * CSTR + 4 + c * Q: (16 + i) * CSTR + 4 + c * Q + Q],
                        inc=ck_t if i == 1 else None, waits=w)
        tr_state[("pt", 0)] = vop(lambda e: e.tensor_copy(
            MB[:, BTM0: BTM0 + 256], PT1[:, 0:256]), waits=[(ck_t, tk)])
        if SCANPART <= 2:
            return
        gbanks = [PA, PB, PC, PD, PE1, PF, PT1f, PT2f]
        kb.wait(TEN, ck_v, wave_fin.get(c * 8 - 1))
        kb.wait(TEN, ck_v, tr_state.get(("pt", 0)))
        kb.wait(TEN, ck_v, tr_state.get(("pt", 1)))
        tkg = None
        for g in range(8):
            qt, q4 = g // 4, 32 * (g % 4)
            tkg = mm(gbanks[g][:, 0:128],
                     XBC[q4: q4 + 32, (16 + qt) * CSTR + 4 + c * Q:
                         (16 + qt) * CSTR + 4 + c * Q + Q],
                     XBC[q4: q4 + 32, (18 + qt) * CSTR + 4 + c * Q:
                         (18 + qt) * CSTR + 4 + c * Q + Q],
                     start=True, stop=True, tp=(q4, 0) if q4 == 96 else None,
                     inc=ck_t if g == 7 else None)
        for g in range(8):
            g_ev[0] = vop(lambda e, g=g: e.tensor_mul(
                MF[:, g * 128: g * 128 + 128],
                gbanks[g][:, 0:128], mask[:]),
                waits=[(ck_t, tkg)])
        v_mg = ck_v.n
        if SCANPART <= 3:
            return

        for wv in range(8):
            aw = c * 8 + wv
            bcb = PE1 if wv % 2 == 0 else PF
            y1b = PA if wv % 2 == 0 else PC
            y2b = PB if wv % 2 == 0 else PD
            hs = [wv * 4 + i for i in range(4)]
            kb.wait(TEN, ck_v, v_mg)
            kb.wait(TEN, ck_v, wave_fin.get(aw - 2))
            for i, h in enumerate(hs):
                vop(lambda e, i=i, h=h, c=c: e.tensor_scalar(
                    MB[:, ATB0 + i * 128: ATB0 + i * 128 + 128], onesf[:],
                    DT_TM[:, c * 32 + h: c * 32 + h + 1],
                    NEA_B[:, h: h + 1], OP.mult, OP.mult))
                vop(lambda e, i=i, h=h, c=c: e.tensor_scalar(
                    MF_b[:, 3072 + i * 128: 3072 + i * 128 + 128], onesf[:],
                    DT_TM[:, c * 32 + h: c * 32 + h + 1],
                    PEA_B[:, h: h + 1], OP.mult, OP.mult))
            v_lab = ck_v.n
            kb.wait(TEN, ck_v, v_lab)
            t_bc = None
            for i, h in enumerate(hs):
                mm(bcb[:, i * 128: i * 128 + 128],
                   MB[:, ATB0 + i * 128: ATB0 + i * 128 + 128], MASKB[:],
                   True, False)
                mm(bcb[:, i * 128: i * 128 + 128], MASKB[:],
                   MF_b[:, 3072 + i * 128: 3072 + i * 128 + 128],
                   False, True)
                t_bc = mm(PT1f[:, wv * 4 + i: wv * 4 + i + 1],
                          MB[:, ATB0 + i * 128: ATB0 + i * 128 + 128], onesc[:],
                          True, True, inc=ck_t if i == 3 else None)
            for i, h in enumerate(hs):
                vop(lambda e, bcb=bcb, i=i: e.tensor_scalar(
                    MF[:, 1024 + i * 128: 1024 + i * 128 + 128],
                    bcb[:, i * 128: i * 128 + 128], 0.0, None, OP.min),
                    waits=[(ck_t, t_bc)])
            v_dif = ck_v.n
            if SCANPART <= 4:
                wave_fin[aw] = ck_v.n
                continue
            for i, h in enumerate(hs):
                sop(lambda e, i=i, h=h, c=c, wv=wv: e.activation(
                    GAM[:, c * 32 + h: c * 32 + h + 1],
                    PT1f[:, wv * 4 + i: wv * 4 + i + 1], AF.Exp),
                    waits=[(ck_t, t_bc)])
            for i in range(4):
                sop(lambda e, i=i: e.activation(
                    MF[:, 1536 + i * 128: 1536 + i * 128 + 128],
                    MF[:, 1024 + i * 128: 1024 + i * 128 + 128], AF.Exp),
                    waits=[(ck_v, v_dif)])
            s_ex = ck_s.n
            for i, h in enumerate(hs):
                gg = h // 4
                vop(lambda e, i=i, gg=gg: e.tensor_mul(
                    MB[:, ATB0 + i * 128: ATB0 + i * 128 + 128],
                    MF[:, gg * 128: gg * 128 + 128],
                    MF[:, 1536 + i * 128: 1536 + i * 128 + 128]),
                    waits=[(ck_s, s_ex)])
                vop(lambda e, i=i, h=h, c=c: e.tensor_scalar(
                    MB[:, XT0 + i * 64: XT0 + i * 64 + 64],
                    MB[:, XTM0 + h * 64: XTM0 + h * 64 + 64],
                    DT_TM[:, c * 32 + h: c * 32 + h + 1], None, OP.mult))
                vop(lambda e, i=i: e.tensor_scalar(
                    MB[:, XD0 + i * 64: XD0 + i * 64 + 64],
                    MB[:, XT0 + i * 64: XT0 + i * 64 + 64],
                    MF[:, 1536 + i * 128 + 127: 1536 + i * 128 + 128],
                    None, OP.mult))
            v_at = ck_v.n
            if SCANPART <= 5:
                wave_fin[aw] = ck_v.n
                continue
            kb.wait(TEN, ck_v, v_at)
            t_y = None
            for i, h in enumerate(hs):
                gg = h // 4
                qt, q4 = gg // 4, 32 * (gg % 4)
                mm(y1b[:, i * 64: i * 64 + 64],
                   MB[:, ATB0 + i * 128: ATB0 + i * 128 + 128],
                   MB[:, XT0 + i * 64: XT0 + i * 64 + 64], True, True)
                mm(y2b[:, i * 64: i * 64 + 64],
                   XBC[q4: q4 + 32, (18 + qt) * CSTR + 4 + c * Q:
                       (18 + qt) * CSTR + 4 + c * Q + Q],
                   SST[q4: q4 + 32, qt * 256 + (h % 4) * 64: qt * 256 + (h % 4) * 64 + 64],
                   True, True, tp=(q4, 0) if q4 == 96 else None)
                t_y = mm(y1b[q4: q4 + 32, 256 + i * 64: 256 + i * 64 + 64],
                         MB[:, BTM0 + gg * 32: BTM0 + gg * 32 + 32],
                         MB[:, XD0 + i * 64: XD0 + i * 64 + 64],
                         True, True, tp=(0, q4),
                         inc=ck_t if i == 3 else None)
            for i, h in enumerate(hs):
                gg = h // 4
                qt, q4 = gg // 4, 32 * (gg % 4)
                vop(lambda e, y2b=y2b, i=i, h=h, c=c: e.tensor_scalar(
                    MF[:, 2048 + i * 64: 2048 + i * 64 + 64],
                    y2b[:, i * 64: i * 64 + 64],
                    EIN_TM[:, c * 32 + h: c * 32 + h + 1], None, OP.mult),
                    waits=[(ck_t, t_y)])
                vop(lambda e, y1b=y1b, i=i, h=h: e.scalar_tensor_tensor(
                    MF[:, 2304 + i * 64: 2304 + i * 64 + 64],
                    MB[:, XTM0 + h * 64: XTM0 + h * 64 + 64],
                    dbc[:, h: h + 1], y1b[:, i * 64: i * 64 + 64],
                    OP.mult, OP.add))
                vop(lambda e, i=i, h=h: e.tensor_add(
                    MB[:, YTMB0 + h * 64: YTMB0 + h * 64 + 64],
                    MF[:, 2048 + i * 64: 2048 + i * 64 + 64],
                    MF[:, 2304 + i * 64: 2304 + i * 64 + 64]))
                vop(lambda e, y1b=y1b, i=i, h=h, qt=qt, q4=q4, c=c: e.scalar_tensor_tensor(
                    SST[q4: q4 + 32, qt * 256 + (h % 4) * 64: qt * 256 + (h % 4) * 64 + 64],
                    SST[q4: q4 + 32, qt * 256 + (h % 4) * 64: qt * 256 + (h % 4) * 64 + 64],
                    GAM[q4: q4 + 32, c * 32 + h: c * 32 + h + 1],
                    y1b[q4: q4 + 32, 256 + i * 64: 256 + i * 64 + 64],
                    OP.mult, OP.add))
            wave_fin[aw] = ck_v.n
        if SCANPART <= 6:
            return
        v_y = ck_v.n
        for wv in range(2):
            pt = PT1 if wv == 0 else PT2
            w = [(ck_v, tr_state[("pt", wv)]), (ck_v, v_y)]
            tk = None
            for i in range(8):
                f = wv * 8 + i
                tk = transp(pt[:, i * 128: i * 128 + 128],
                            MB[:, YTMB0 + f * 128: YTMB0 + f * 128 + 128],
                            inc=ck_t if i == 7 else None, waits=w if i == 0 else ())
            last = None
            for i in range(8):
                f = wv * 8 + i
                last = vop(lambda e, pt=pt, i=i, f=f, c=c: e.tensor_copy(
                    UB[:, f * T + c * Q: f * T + c * Q + Q],
                    pt[:, i * 128: i * 128 + 128]), waits=[(ck_t, tk)])
            tr_state[("pt", wv)] = last

    def mamba_state_exchange():
        barrier()
        dst_tok = adma(st_src.ap(), SST[:])
        kb.op(GPS, lambda e: e.collective_compute(
            "AllGather", OP.bypass, replica_groups=PAIR_GROUPS,
            ins=[st_src.ap()], outs=[st_dst.ap()]),
            inc=ck_cc, waits=[dst_tok, (ck_g, ck_g.n)])
        adma(SSIB[:], st_dst[0], waits=[(ck_cc, ck_cc.n)])
        barrier()

    def mamba_sib_correction():
        corr_tr = {}
        for c in range(NCH):
            t_y = None
            vfirst = {}
            for h in range(H):
                gg = h // 4
                qt, q4 = gg // 4, 32 * (gg % 4)
                bank = PA if h % 2 == 0 else PB
                kb.wait(TEN, ck_v, vfirst.get(h - 2))
                if h < 2:
                    kb.wait(TEN, ck_v, corr_tr.get("last", None))
                t_y = mm(bank[:, 0:64],
                         XBC[q4: q4 + 32, (18 + qt) * CSTR + 4 + c * Q:
                             (18 + qt) * CSTR + 4 + c * Q + Q],
                         SSIB[q4: q4 + 32, qt * 256 + (h % 4) * 64:
                              qt * 256 + (h % 4) * 64 + 64],
                         True, True, tp=(q4, 0) if q4 == 96 else None, inc=ck_t)
                vfirst[h] = vop(lambda e, bank=bank, h=h, c=c: e.tensor_scalar(
                    MB[:, YTMB0 + h * 64: YTMB0 + h * 64 + 64],
                    bank[:, 0:64], EG_TM[:, c * 32 + h: c * 32 + h + 1],
                    None, OP.mult), waits=[(ck_t, t_y)])
            v_fin = ck_v.n
            for wv in range(2):
                pt = PT1 if wv == 0 else PT2
                w = [(ck_v, corr_tr.get(wv)), (ck_v, v_fin)]
                tk = None
                for i in range(8):
                    f = wv * 8 + i
                    tk = transp(pt[:, i * 128: i * 128 + 128],
                                MB[:, YTMB0 + f * 128: YTMB0 + f * 128 + 128],
                                inc=ck_t if i == 7 else None, waits=w if i == 0 else ())
                last = None
                for i in range(8):
                    f = wv * 8 + i
                    last = vop(lambda e, pt=pt, i=i, f=f, c=c: e.tensor_add(
                        UB[:, f * T + c * Q: f * T + c * Q + Q],
                        UB[:, f * T + c * Q: f * T + c * Q + Q],
                        pt[:, i * 128: i * 128 + 128]), waits=[(ck_t, tk)])
                corr_tr[wv] = last
            corr_tr["last"] = ck_v.n
        barrier()

    def mamba_gate_norm_outproj():
        for f in range(FT):
            vop(lambda e, f=f: e.tensor_mul(UB[:, f * T: f * T + T],
                                            UB[:, f * T: f * T + T],
                                            YTM[:, f * T: f * T + T]))
        v_g = ck_v.n
        sq_s, mmt = {}, {}
        for f in range(FT):
            w = [(ck_v, v_g)]
            if f >= 2:
                w.append((ck_t, mmt.get(f - 2)))
            sq_s[f] = sop(lambda e, f=f: e.activation(
                MB[:, (f % 2) * 1024: (f % 2) * 1024 + 1024],
                UB[:, f * T: f * T + T], AF.Square), waits=w)
            for th in range(2):
                bank = PE1 if th == 0 else PF
                mmt[f] = mm(bank[0:8, 0:512], GON[:, f * 8: f * 8 + 8],
                            MB[:, (f % 2) * 1024 + th * 512: (f % 2) * 1024 + th * 512 + 512],
                            start=(f == 0), stop=(f == FT - 1),
                            inc=ck_t, waits=[(ck_s, sq_s[f])])
        t_var = ck_t.n
        vop(lambda e: e.tensor_copy(VARB[:, 0:512], PE1[0:8, 0:512]), waits=[(ck_t, t_var)])
        vop(lambda e: e.tensor_copy(VARB[:, 512:1024], PF[0:8, 0:512]))
        vop(lambda e: e.tensor_scalar(RSTD[:], VARB[:], 1.0 / 256.0, EPS, OP.mult, OP.add))
        v_r = ck_v.n
        sop(lambda e: e.activation(RSTD[:], RSTD[:], AF.Sqrt), waits=[(ck_v, v_r)])
        s_r = ck_s.n
        vop(lambda e: e.reciprocal(RSTD[:], RSTD[:]), waits=[(ck_s, s_r)])
        vop(lambda e: e.tensor_copy(RSTDB, RSTD))
        v_rr = ck_v.n
        gop(lambda e: e.memset(SELG, 0.0), waits=[(ck_t, t_var)])
        gop(lambda e: e.affine_select(
            out=SELG.rearrange("p (g m) -> p g m", g=8),
            in_=SELG.rearrange("p (g m) -> p g m", g=8),
            compare_op=OP.not_equal, fill=1.0, base=0,
            pattern=[[-1, 8], [0, 128]], channel_multiplier=1))
        g_sel = ck_g.n
        ap_ev = {}
        for g in range(8):
            for th in range(2):
                bank = PE1 if (g * 2 + th) % 2 == 0 else PF
                kb.wait(TEN, ck_v, v_rr)
                kb.wait(TEN, ck_v, ap_ev.get(g * 2 + th - 2))
                kb.wait(TEN, ck_g, g_sel)
                tkb = mm(bank[:, 0:512], SELG[:, g * 128: g * 128 + 128],
                         RSTDB[:, th * 512: th * 512 + 512], True, True,
                         inc=ck_t)
                last = None
                for sub in range(2):
                    f = 2 * g + sub
                    last = vop(lambda e, f=f, th=th, bank=bank: e.scalar_tensor_tensor(
                        UB[:, f * T + th * 512: f * T + th * 512 + 512],
                        UB[:, f * T + th * 512: f * T + th * 512 + 512],
                        normw_t[:, f: f + 1], bank[:, 0:512],
                        OP.mult, OP.mult), waits=[(ck_t, tkb)])
                ap_ev[g * 2 + th] = last
        barrier()
        blk = FT * 128
        mm_done, epi_done = {}, {}
        for oi in range(FT):
            bsl = WS[:, (oi % 2) * blk: (oi % 2) * blk + blk]
            w = [(ck_t, mm_done.get(oi - 2))] if oi >= 2 else []
            dtok = wdma(bsl.rearrange("p (kt o) -> p kt o", kt=FT),
                        W_out[:, oi * 128: oi * 128 + 128]
                        .rearrange("(kt p) o -> p kt o", p=128), waits=w)
            tk = None
            for th in range(2):
                bank = (PA, PB, PC, PD)[(oi * 2 + th) % 4]
                kb.wait(TEN, dtok[0], dtok[1])
                if oi * 2 + th >= 4:
                    kb.wait(TEN, ck_v, epi_done.get(oi * 2 + th - 4))
                for kt in range(FT):
                    tk = mm(bank[:, 0:512], bsl[:, kt * 128: kt * 128 + 128],
                            UB[:, kt * T + th * 512: kt * T + th * 512 + 512],
                            start=(kt == 0), stop=(kt == FT - 1),
                            inc=ck_t if kt == FT - 1 else None)
                epi_done[oi * 2 + th] = vop(
                    lambda e, oi=oi, th=th, bank=bank: e.scalar_tensor_tensor(
                        RES[:, oi * T + th * 512: oi * T + th * 512 + 512],
                        bank[:, 0:512], bout_t[:, oi: oi + 1],
                        RES[:, oi * T + th * 512: oi * T + th * 512 + 512],
                        OP.add, OP.add), waits=[(ck_t, tk)])
            mm_done[oi] = tk
        barrier()

    def mamba():
        v_ubnd = mamba_boundary_ln()
        mamba_inproj(v_ubnd)
        if STAGE <= 1:
            barrier()
            return
        mamba_conv()
        if STAGE <= 2:
            barrier()
            return
        mamba_rows()
        if STAGE <= 3:
            barrier()
            return
        wave_fin, tr_state, g_ev = {}, {}, {}
        for c in range(min(NCH, STAGE - 3)):
            scan_chunk(c, wave_fin, tr_state, g_ev)
        if SCAN_ONLY or STAGE < 99:
            barrier()
            return
        mamba_state_exchange()
        mamba_sib_correction()
        mamba_gate_norm_outproj()

    def moe_gate():
        for th in range(2):
            bank = PE1 if th == 0 else PF
            tk = None
            for kt in range(FT):
                tk = mm(bank[0:8, 0:512], gwt[:, kt * 8: kt * 8 + 8],
                        UB[:, kt * T + th * 512: kt * T + th * 512 + 512],
                        start=(kt == 0), stop=(kt == FT - 1),
                        inc=ck_t if kt == FT - 1 else None)
        t_lg = ck_t.n
        vop(lambda e: e.tensor_copy(MF[0:8, 0:512], PE1[0:8, 0:512]), waits=[(ck_t, t_lg)])
        vop(lambda e: e.tensor_copy(MF[0:8, 512:1024], PF[0:8, 0:512]))
        v_lg = ck_v.n
        ltr = {}
        for tt in range(8):
            pf = PT1f if tt % 2 == 0 else PT2f
            w = [(ck_v, v_lg)]
            if tt >= 2:
                w.append((ck_v, ltr[tt - 2]))
            tk = transp(pf[0:128, 0:8], MF[0:8, tt * 128: tt * 128 + 128],
                        inc=ck_t, waits=w)
            ltr[tt] = vop(lambda e, pf=pf, tt=tt: e.tensor_copy(
                LGTM[:, tt * 8: tt * 8 + 8], pf[0:128, 0:8]), waits=[(ck_t, tk)])
        for tt in range(8):
            base = tt * 8
            vop(lambda e, base=base: e.reduce_max(GSC[:, 64:65], LGTM[:, base: base + 8],
                                                  axis=AX.X, negate=True))
            v_m = ck_v.n
            sop(lambda e, base=base: e.activation(GSC[:, 68:76], LGTM[:, base: base + 8],
                                                  AF.Exp, bias=GSC[:, 64:65]),
                waits=[(ck_v, v_m)])
            s_x = ck_s.n
            vop(lambda e: e.reduce_max(GSC[:, 65:66], GSC[:, 68:76], axis=AX.X),
                waits=[(ck_s, s_x)])
            vop(lambda e: e.tensor_scalar(GSC[:, 76:84], GSC[:, 68:76],
                                          GSC[:, 65:66], None, OP.is_equal))
            vop(lambda e: e.scalar_tensor_tensor(GSC[:, 76:84], GSC[:, 76:84],
                                                 -1e30, GSC[:, 68:76], OP.mult, OP.add))
            vop(lambda e: e.reduce_max(GSC[:, 66:67], GSC[:, 76:84], axis=AX.X))
            vop(lambda e: e.tensor_scalar(GSC[:, 76:84], GSC[:, 68:76],
                                          GSC[:, 66:67], None, OP.is_ge))
            vop(lambda e: e.tensor_mul(GSC[:, 76:84], GSC[:, 76:84], GSC[:, 68:76]))
            vop(lambda e: e.reduce_sum(GSC[:, 67:68], GSC[:, 76:84], axis=AX.X))
            vop(lambda e: e.reciprocal(GSC[:, 67:68], GSC[:, 67:68]))
            vop(lambda e, tt=tt: e.tensor_scalar(COEF[:, tt * 8: tt * 8 + 8],
                                                 GSC[:, 76:84], GSC[:, 67:68],
                                                 None, OP.mult))
        barrier()

    def moe_mlp(e_i, srcw1, srcw3, srcw2, i0, nI, accum_mode, st):
        blkc = FT * 128
        for it in range(nI):
            git = st["git"]
            st["git"] += 1
            b1 = WS[:, (git % 2) * blkc: (git % 2) * blkc + blkc]
            b3 = WS[:, 4096 + (git % 2) * blkc: 4096 + (git % 2) * blkc + blkc]
            w = [(ck_t, st["hg_mm"].get(git - 2))]
            d1 = wdma(b1.rearrange("p (kt o) -> p kt o", kt=FT),
                      srcw1[:, (i0 + it) * 128: (i0 + it) * 128 + 128]
                      .rearrange("(kt p) o -> p kt o", p=128), waits=w)
            d3 = wdma(b3.rearrange("p (kt o) -> p kt o", kt=FT),
                      srcw3[:, (i0 + it) * 128: (i0 + it) * 128 + 128]
                      .rearrange("(kt p) o -> p kt o", p=128))
            kb.wait(TEN, d1[0], d1[1])
            kb.wait(TEN, d3[0], d3[1])
            kb.wait(TEN, ck_v, st["gt_v"].get(git - 2))
            bset = ((PA, PB, PC, PD), (PE1, PF, PT1f, PT2f))[git % 2]
            tkh = tkg = None
            for th in range(2):
                hb = bset[th]
                gb = bset[2 + th]
                for kt in range(FT):
                    tkh = mm(hb[:, 0:512], b1[:, kt * 128: kt * 128 + 128],
                             UB[:, kt * T + th * 512: kt * T + th * 512 + 512],
                             start=(kt == 0), stop=(kt == FT - 1),
                             inc=ck_t if kt == FT - 1 else None)
                for kt in range(FT):
                    tkg = mm(gb[:, 0:512], b3[:, kt * 128: kt * 128 + 128],
                             UB[:, kt * T + th * 512: kt * T + th * 512 + 512],
                             start=(kt == 0), stop=(kt == FT - 1),
                             inc=ck_t if kt == FT - 1 else None)
                ss = sop(lambda e, th=th, hb=hb, git=git: e.activation(
                    HSB[:, (git % 2) * 1024 + th * 512: (git % 2) * 1024 + th * 512 + 512],
                    hb[:, 0:512], AF.Silu), waits=[(ck_t, tkh)])
                st["gt_v"][git] = vop(lambda e, th=th, gb=gb, it=it, git=git: e.tensor_mul(
                    XBC[:, it * CSTR + th * 512: it * CSTR + th * 512 + 512],
                    HSB[:, (git % 2) * 1024 + th * 512: (git % 2) * 1024 + th * 512 + 512],
                    gb[:, 0:512]), waits=[(ck_s, ss), (ck_t, tkg)])
            st["hg_mm"][git] = tkg
        v_gt = ck_v.n
        for oc in range(8):
            goc = st["goc"]
            st["goc"] += 1
            wb = WS[:, 8192 + (goc % 2) * 2816: 8192 + (goc % 2) * 2816 + 2816]
            w = [(ck_t, st["y_mm"].get(goc - 2))]
            dtok = wdma(wb.rearrange("p (it o) -> p it o", it=nI),
                        srcw2[i0 * 128: i0 * 128 + nI * 128, oc * 256: oc * 256 + 256]
                        .rearrange("(it p) o -> p it o", p=128), waits=w)
            kb.wait(TEN, dtok[0], dtok[1])
            kb.wait(TEN, ck_v, v_gt)
            tk = None
            for tt in range(8):
                bank = (PA, PB, PC, PD)[tt % 4]
                if tt >= 4 or oc > 0:
                    kb.wait(TEN, ck_v, st["ev_v"].get(goc * 8 + tt - 4))
                elif oc == 0:
                    kb.wait(TEN, ck_v, v_gt)
                for it in range(nI):
                    tk = mm(bank[:, 0:256],
                            XBC[:, it * CSTR + tt * 128: it * CSTR + tt * 128 + 128],
                            wb[:, it * 256: it * 256 + 256],
                            start=(it == 0), stop=(it == nI - 1),
                            inc=ck_t if it == nI - 1 else None)
                dst = YTM[:, tt * 2048 + oc * 256: tt * 2048 + oc * 256 + 256]
                if accum_mode == "coef0":
                    st["ev_v"][goc * 8 + tt] = vop(
                        lambda e, dst=dst, bank=bank, tt=tt, e_i=e_i: e.tensor_scalar(
                            dst, bank[:, 0:256],
                            COEF[:, tt * 8 + e_i: tt * 8 + e_i + 1], None, OP.mult),
                        waits=[(ck_t, tk)])
                elif accum_mode == "coef":
                    st["ev_v"][goc * 8 + tt] = vop(
                        lambda e, dst=dst, bank=bank, tt=tt, e_i=e_i: e.scalar_tensor_tensor(
                            dst, bank[:, 0:256],
                            COEF[:, tt * 8 + e_i: tt * 8 + e_i + 1], dst,
                            OP.mult, OP.add), waits=[(ck_t, tk)])
                else:
                    st["ev_v"][goc * 8 + tt] = vop(
                        lambda e, dst=dst, bank=bank: e.tensor_tensor(
                            dst, dst, bank[:, 0:256], OP.add), waits=[(ck_t, tk)])
            st["y_mm"][goc] = tk

    def moe():
        moe_gate()
        if MOESTAGE <= 1:
            barrier()
            return
        st = {"git": 0, "goc": 0, "hg_mm": {}, "gt_v": {}, "y_mm": {}, "ev_v": {}}
        for e_i in range(E_MOE if MOESTAGE > 2 else 1):
            kb.wait(TEN, ck_v, ck_v.n)
            moe_mlp(e_i, w1[e_i], w3[e_i], w2[e_i], 0, IT_MOE,
                    "coef0" if e_i == 0 else "coef", st)
        if MOESTAGE <= 3:
            barrier()
            return
        for half in range(2):
            kb.wait(TEN, ck_v, ck_v.n)
            moe_mlp(0, sw1, sw3, sw2, half * IT_MOE, IT_MOE, "add", st)
        barrier()
        trp = {}
        for tt in range(8):
            for wv in range(2):
                pt = PT1 if wv == 0 else PT2
                w = [(ck_v, trp.get((tt - 1, wv)))]
                tk = None
                for i in range(8):
                    f = wv * 8 + i
                    tk = transp(pt[:, i * 128: i * 128 + 128],
                                YTM[:, tt * 2048 + f * 128: tt * 2048 + f * 128 + 128],
                                inc=ck_t if i == 7 else None, waits=w if i == 0 else ())
                last = None
                for i in range(8):
                    f = wv * 8 + i
                    last = vop(lambda e, pt=pt, i=i, f=f, tt=tt: e.tensor_add(
                        RES[:, f * T + tt * 128: f * T + tt * 128 + 128],
                        RES[:, f * T + tt * 128: f * T + tt * 128 + 128],
                        pt[:, i * 128: i * 128 + 128]), waits=[(ck_t, tk)])
                trp[(tt, wv)] = last
        barrier()

    def attn_qkv():
        blk = FT * 128
        mm_done, ev_done, st_done = {}, {}, {}
        for oi in range(48):
            bsl = WS[:, (oi % 2) * blk: (oi % 2) * blk + blk]
            w = [(ck_t, mm_done.get(oi - 2))] if oi >= 2 else []
            dtok = wdma(bsl.rearrange("p (kt o) -> p kt o", kt=FT),
                        WqkvT[:, oi * 128: oi * 128 + 128]
                        .rearrange("(kt p) o -> p kt o", p=128), waits=w)
            h, wq = oi // 3, oi % 3
            tk = None
            for th in range(2):
                bank = (PA, PB, PC, PD)[(oi * 2 + th) % 4]
                kb.wait(TEN, dtok[0], dtok[1])
                if oi * 2 + th >= 4:
                    kb.wait(TEN, ck_v, ev_done.get(oi * 2 + th - 4))
                for kt in range(FT):
                    tk = mm(bank[:, 0:512], bsl[:, kt * 128: kt * 128 + 128],
                            UB[:, kt * T + th * 512: kt * T + th * 512 + 512],
                            start=(kt == 0), stop=(kt == FT - 1),
                            inc=ck_t if kt == FT - 1 else None)
                wl = [(ck_t, tk)]
                if st_done.get(oi - 2) is not None:
                    wl.append(st_done[oi - 2])
                ev_done[oi * 2 + th] = vop(
                    lambda e, oi=oi, th=th, bank=bank: e.tensor_copy(
                        BOU[:, (oi % 2) * T + th * 512: (oi % 2) * T + th * 512 + 512],
                        bank[:, 0:512]), waits=wl)
            mm_done[oi] = tk
            dst = qdram[h] if wq == 0 else kv_src[h, wq - 1]
            st_done[oi] = adma(dst, BOU[:, (oi % 2) * T: (oi % 2) * T + T],
                               waits=[(ck_v, ev_done[oi * 2 + 1])])
        barrier()
        kb.op(GPS, lambda e: e.collective_compute(
            "AllGather", OP.bypass, replica_groups=PAIR_GROUPS,
            ins=[kv_src.ap()], outs=[kv_dst.ap()]),
            inc=ck_cc, waits=[(c, c.n) for c in ad_clks] + [(ck_g, ck_g.n)])
        barrier()

    def attn_core():
        hload, last_mm = {}, {}
        attn_pt, attn_ex = {}, {}
        v_ctx_ev = {}

        def load_head(h):
            b = h % 2
            w = [(ck_t, last_mm.get(h - 2))]
            toks = [adma(QH[:, b * T: b * T + T], qdram[h], waits=w)]
            for sl in range(2):
                toks.append(adma(KH[:, b * 2048 + sl * T: b * 2048 + sl * T + T],
                                 kv_dst[sl, h, 0]))
                toks.append(adma(VH[:, b * 2048 + sl * T: b * 2048 + sl * T + T],
                                 kv_dst[sl, h, 1]))
            hload[h] = toks

        load_head(0)
        load_head(1)
        for h in range(NH):
            b = h % 2
            for tok in hload[h]:
                kb.wait(TEN, tok[0], tok[1])
            for wv in range(2):
                pt = PT1 if wv == 0 else PT2
                w = [(ck_v, attn_pt.get(wv))]
                tk = None
                for i in range(8):
                    j = wv * 8 + i
                    tk = transp(pt[:, i * 128: i * 128 + 128],
                                VH[:, b * 2048 + j * 128: b * 2048 + j * 128 + 128],
                                inc=ck_t if i == 7 else None, waits=w if i == 0 else ())
                attn_pt[wv] = vop(lambda e, pt=pt, wv=wv: e.tensor_copy(
                    VTM[:, wv * 1024: wv * 1024 + 1024], pt[:]), waits=[(ck_t, tk)])
            kb.wait(TEN, ck_v, v_ctx_ev.get(h - 1))
            t0 = mm(PE1[:, 0:512], KH[:, b * 2048: b * 2048 + 128],
                    QH[:, b * T: b * T + 512], True, True, inc=ck_t)
            vop(lambda e: e.reduce_max(MXB[:, 0:1], PE1[:, 0:512], axis=AX.X),
                waits=[(ck_t, t0)])
            v_m1 = ck_v.n
            tmx = transp(PT1[0:1, 0:128], MXB[:, 0:1], inc=ck_t,
                         waits=[(ck_v, attn_pt[0]), (ck_v, v_m1)])
            vop(lambda e: e.reduce_max(MXB[0:1, 1:2], PT1[0:1, 0:128], axis=AX.X),
                waits=[(ck_t, tmx)])
            v_m2 = ck_v.n
            tcc = mm(PF[:, 0:1], onesrb[:], MXB[0:1, 1:2], True, True, inc=ck_t,
                     waits=[(ck_v, v_m2)])
            v_cc = vop(lambda e: e.tensor_scalar(CC[:], PF[:, 0:1], -RSQ, None,
                                                 OP.mult), waits=[(ck_t, tcc)])
            l_mm, l_ev = {}, {}
            for ih in range(2):
                for jt in range(16):
                    gjt = ih * 16 + jt
                    sb = PA if jt % 2 == 0 else PB
                    kb.wait(TEN, ck_s, attn_ex.get(gjt - 2))
                    if gjt == 0:
                        kb.wait(TEN, ck_v, v_cc)
                    ts = mm(sb[:, 0:512],
                            KH[:, b * 2048 + jt * 128: b * 2048 + jt * 128 + 128],
                            QH[:, b * T + ih * 512: b * T + ih * 512 + 512], True, True,
                            inc=ck_t)
                    wl = [(ck_t, ts)]
                    if ih == 1:
                        wl.append((ck_t, l_mm.get(0)))
                    attn_ex[gjt] = sop(
                        lambda e, sb=sb, jt=jt: e.activation(
                            UB[:, jt * 512: jt * 512 + 512], sb[:, 0:512], AF.Exp,
                            bias=CC[:, 0:1], scale=RSQ), waits=wl)
                    mm(PC[:, 0:512] if ih == 0 else PD[:, 0:512],
                       VTM[:, jt * 128: jt * 128 + 128],
                       UB[:, jt * 512: jt * 512 + 512],
                       start=(jt == 0), stop=(jt == 15),
                       inc=ck_t if jt == 15 else None,
                       waits=[(ck_s, attn_ex[gjt])])
                t_ctx = ck_t.n
                tl = None
                for jt in range(16):
                    if jt == 0:
                        kb.wait(TEN, ck_v, l_ev.get(ih - 1))
                    tl = mm(PE1[0:1, 0:512], onesc[:], UB[:, jt * 512: jt * 512 + 512],
                            start=(jt == 0), stop=(jt == 15),
                            inc=ck_t if jt == 15 else None)
                l_mm[ih] = tl
                l_ev[ih] = vop(lambda e, ih=ih: e.tensor_copy(
                    LROW[0:1, ih * 512: ih * 512 + 512],
                    PE1[0:1, 0:512]), waits=[(ck_t, tl)])
                if ih == 0:
                    t_ctx0 = t_ctx
                else:
                    t_ctx1 = t_ctx
            vop(lambda e: e.reciprocal(LROW[:], LROW[:]))
            v_li = ck_v.n
            li_ev = {}
            for ih in range(2):
                tb = mm(PF[:, 0:512], onesf[0:1, 0:128],
                        LROW[0:1, ih * 512: ih * 512 + 512], True, True, inc=ck_t,
                        waits=[(ck_v, v_li), (ck_v, li_ev.get(ih - 1))])
                li_ev[ih] = vop(lambda e, ih=ih: e.tensor_copy(
                    LINV[:, ih * 512: ih * 512 + 512],
                    PF[:, 0:512]), waits=[(ck_t, tb)])
            v_lb = ck_v.n
            last = None
            for ih in range(2):
                ctxb = PC if ih == 0 else PD
                last = vop(lambda e, h=h, ih=ih, ctxb=ctxb: e.tensor_mul(
                    YTM[:, h * T + ih * 512: h * T + ih * 512 + 512],
                    ctxb[:, 0:512], LINV[:, ih * 512: ih * 512 + 512]),
                    waits=[(ck_t, t_ctx0 if ih == 0 else t_ctx1), (ck_v, v_lb)])
            v_ctx_ev[h] = last
            last_mm[h] = ck_t.n
            adma(ctx_src[h], YTM[:, h * T: h * T + T], waits=[(ck_v, last)])
            if h + 2 < NH:
                load_head(h + 2)
        barrier()
        kb.op(GPS, lambda e: e.collective_compute(
            "AllGather", OP.bypass,
            replica_groups=[[0, 2, 4, 6], [1, 3, 5, 7]],
            ins=[ctx_src.ap()], outs=[ctx_dst.ap()]),
            inc=ck_cc, waits=[(c, c.n) for c in ad_clks] + [(ck_g, ck_g.n)])
        barrier()
        sel_v = {}
        for np_ in range(NH):
            toks = []
            for j in range(4):
                toks.append(adma(XBC[:, j * 1024: j * 1024 + 1024],
                                 ctx_dst[np_ // 4, 4 * (np_ % 4) + j],
                                 waits=[(ck_v, sel_v.get(np_ - 1))]))
            for th in range(2):
                last = vop(lambda e, np_=np_, th=th: e.tensor_scalar(
                    YTM[:, np_ * T + th * 512: np_ * T + th * 512 + 512],
                    XBC[:, th * 512: th * 512 + 512],
                    bsel_t[:, 0:1], None, OP.mult), waits=toks)
                for j in range(1, 4):
                    last = vop(lambda e, np_=np_, th=th, j=j: e.scalar_tensor_tensor(
                        YTM[:, np_ * T + th * 512: np_ * T + th * 512 + 512],
                        XBC[:, j * 1024 + th * 512: j * 1024 + th * 512 + 512],
                        bsel_t[:, j: j + 1],
                        YTM[:, np_ * T + th * 512: np_ * T + th * 512 + 512],
                        OP.mult, OP.add))
            sel_v[np_] = ck_v.n
        barrier()

    def attn_proj():
        blk = FT * 128
        mm_done, epi_done = {}, {}
        for oi in range(FT):
            bsl = WS[:, (oi % 2) * blk: (oi % 2) * blk + blk]
            w = [(ck_t, mm_done.get(oi - 2))] if oi >= 2 else []
            dtok = wdma(bsl.rearrange("p (kt o) -> p kt o", kt=FT),
                        WprojT[:, oi * 128: oi * 128 + 128]
                        .rearrange("(kt p) o -> p kt o", p=128), waits=w)
            tk = None
            for th in range(2):
                bank = (PA, PB, PC, PD)[(oi * 2 + th) % 4]
                kb.wait(TEN, dtok[0], dtok[1])
                if oi * 2 + th >= 4:
                    kb.wait(TEN, ck_v, epi_done.get(oi * 2 + th - 4))
                for kt in range(FT):
                    tk = mm(bank[:, 0:512], bsl[:, kt * 128: kt * 128 + 128],
                            YTM[:, kt * T + th * 512: kt * T + th * 512 + 512],
                            start=(kt == 0), stop=(kt == FT - 1),
                            inc=ck_t if kt == FT - 1 else None)
                epi_done[oi * 2 + th] = vop(
                    lambda e, oi=oi, th=th, bank=bank: e.scalar_tensor_tensor(
                        RES[:, oi * T + th * 512: oi * T + th * 512 + 512],
                        bank[:, 0:512], bprj_t[:, oi: oi + 1],
                        RES[:, oi * T + th * 512: oi * T + th * 512 + 512],
                        OP.add, OP.add), waits=[(ck_t, tk)])
            mm_done[oi] = tk
        barrier()

    def attention():
        attn_qkv()
        attn_core()
        attn_proj()

    def program():
        ROT["wd"] = 0
        ROT["ad"] = 0
        setup()
        if NSUB >= 1:
            layernorm()
            mamba()
        if NSUB >= 2:
            layernorm()
            moe()
        if NSUB >= 3:
            layernorm()
            attention()
        if NSUB >= 4:
            layernorm()
            moe()
        barrier()
        wdma(out_fm.rearrange("(f p) t -> p f t", p=128),
             RES[:].rearrange("p (f t) -> p f t", f=FT))
        for eng in (TEN, VEC, SCA, GPS, SYN):
            wait_all_dma(eng)

    with nc.Block() as block:
        @block.tensor
        def _(e):
            kb.begin(TEN, e)
            program()

        @block.vector
        def _(e):
            kb.begin(VEC, e)
            program()

        @block.scalar
        def _(e):
            kb.begin(SCA, e)
            program()

        @block.gpsimd
        def _(e):
            kb.begin(GPS, e)
            program()

        @block.sync
        def _(e):
            kb.begin(SYN, e)
            program()

    es.close()
    return nc


_NC_CACHE = {}


def _prep_in_maps(inputs):
    x = np.asarray(inputs["L0_attn_a_hidden_states"], dtype=np.float32)
    shared = {
        "W_in": np.ascontiguousarray(inputs["L0_attn_a_W_in"], np.float32),
        "b_in": np.ascontiguousarray(inputs["L0_attn_a_b_in"], np.float32),
        "mup_vec": np.ascontiguousarray(inputs["L0_attn_a_mup_vector"], np.float32),
        "conv_w": np.ascontiguousarray(inputs["L0_attn_a_conv_w"], np.float32),
        "conv_b": np.ascontiguousarray(inputs["L0_attn_a_conv_b"], np.float32),
        "A_log": np.ascontiguousarray(inputs["L0_attn_a_A_log"], np.float32),
        "D_par": np.ascontiguousarray(inputs["L0_attn_a_D_param"], np.float32),
        "dt_bias": np.ascontiguousarray(inputs["L0_attn_a_dt_bias"], np.float32),
        "norm_w": np.ascontiguousarray(inputs["L0_attn_a_norm_w"], np.float32),
        "W_out": np.ascontiguousarray(inputs["L0_attn_a_W_out"], np.float32),
        "b_out": np.ascontiguousarray(inputs["L0_attn_a_b_out"], np.float32),
        "WqkvT": np.ascontiguousarray(np.asarray(inputs["L0_attn_b_Wqkv"], np.float32).T),
        "WprojT": np.ascontiguousarray(np.asarray(inputs["L0_attn_b_Wproj"], np.float32).T),
        "bproj": np.ascontiguousarray(inputs["L0_attn_b_bproj"], np.float32),
        "gate_wT": np.ascontiguousarray(np.asarray(inputs["L0_ffn_gate_w"], np.float32).T),
        "w1": np.ascontiguousarray(inputs["L0_ffn_w1"], np.float32),
        "w3": np.ascontiguousarray(inputs["L0_ffn_w3"], np.float32),
        "w2": np.ascontiguousarray(inputs["L0_ffn_w2"], np.float32),
        "sw1": np.ascontiguousarray(inputs["L0_ffn_sw1"], np.float32),
        "sw3": np.ascontiguousarray(inputs["L0_ffn_sw3"], np.float32),
        "sw2": np.ascontiguousarray(inputs["L0_ffn_sw2"], np.float32),
    }
    def packo(v, width):
        v = np.asarray(v, np.float32).ravel()
        out = np.zeros((128, width), np.float32)
        n = v.size // 128
        out[:, :n] = v[: n * 128].reshape(n, 128).T
        rem = v.size - n * 128
        if rem:
            out[:rem, n] = v[n * 128:]
        return out

    shared["bin_h"] = packo(inputs["L0_attn_a_b_in"], 37)
    shared["mup_h"] = packo(inputs["L0_attn_a_mup_vector"], 37)
    cw = np.asarray(inputs["L0_attn_a_conv_w"], np.float32).reshape(20, 128, 4)
    shared["convw_h"] = np.ascontiguousarray(
        cw.transpose(1, 0, 2).reshape(128, 80))
    shared["convb_h"] = packo(inputs["L0_attn_a_conv_b"], 20)
    shared["normw_h"] = packo(inputs["L0_attn_a_norm_w"], 16)
    shared["bout_h"] = packo(inputs["L0_attn_a_b_out"], 16)
    shared["bprj_h"] = packo(inputs["L0_attn_b_bproj"], 16)
    shared["a32_h"] = np.stack([np.asarray(inputs["L0_attn_a_A_log"], np.float32),
                                np.asarray(inputs["L0_attn_a_D_param"], np.float32),
                                np.asarray(inputs["L0_attn_a_dt_bias"], np.float32)],
                               axis=1)
    gwt_full = np.asarray(inputs["L0_ffn_gate_w"], np.float32).T
    shared["gwt_h"] = np.ascontiguousarray(
        gwt_full.reshape(16, 128, 8).transpose(1, 0, 2).reshape(128, 128))
    in_maps = []
    for c in range(8):
        b, hh = c // 2, c % 2
        m = dict(shared)
        m["x_fm"] = np.ascontiguousarray(x[b, hh * T:(hh + 1) * T, :].T)
        xb = np.zeros((D, 4), np.float32)
        if hh == 1:
            xb[:, 0:3] = x[b, T - 3: T, :].T
            m["bflag"] = np.ones((128, 1), np.float32)
        else:
            m["bflag"] = np.zeros((128, 1), np.float32)
        m["xbnd_t"] = np.ascontiguousarray(
            xb.reshape(16, 128, 4).transpose(1, 0, 2).reshape(128, 64))
        bs = np.zeros((128, 4), np.float32)
        bs[:, b] = 1.0
        m["bsel"] = bs
        in_maps.append(m)
    return in_maps


def kernel(**inputs):
    if "prog" not in _NC_CACHE:
        _NC_CACHE["prog"] = build_program()
    nc = _NC_CACHE["prog"]
    in_maps = _prep_in_maps(inputs)
    res = run_bass_kernel_spmd(nc, in_maps, core_ids=list(range(8)))
    out = np.empty((B, L, D), np.float32)
    for c in range(8):
        b, hh = c // 2, c % 2
        out[b, hh * T:(hh + 1) * T, :] = np.asarray(res.results[c]["out_fm"]).T
    return out


# version 03 (final)
# speedup vs baseline: 1.1871x; 1.1871x over previous
import numpy as np
from contextlib import ExitStack

import concourse.bass as bass
import concourse.mybir as mybir
from concourse.bass_utils import run_bass_kernel_spmd

F32 = mybir.dt.float32
BF16 = mybir.dt.bfloat16
AF = mybir.ActivationFunctionType
OP = mybir.AluOpType
AX = mybir.AxisListType

B, L, D = 4, 2048, 2048
T = 1024
FT = 16
H = 32
D_SSM = 2048
CONV = 2560
PROJ = 4640
CT = 20
CSTR = 1028
Q = 128
NCH = 8
NH = 16
E_MOE, I_MOE, IT_MOE = 8, 1408, 11
SH_I = 2816
EPS = 1e-5
RSQ = float(1.0 / np.sqrt(np.float32(128.0)))

NSUB = 4
SCAN_ONLY = False
STAGE = 99
SCANPART = 99
MOESTAGE = 99
PAIR_GROUPS = [[0, 1], [2, 3], [4, 5], [6, 7]]


class Clock:
    __slots__ = ("sem", "n", "name")

    def __init__(self, sem, name):
        self.sem, self.n, self.name = sem, 0, name


class KB:
    def __init__(self, nc, es):
        self.nc, self.es = nc, es
        self.clks = {}
        self.cur = None
        self.e = None
        self.marks = {}

    def clk(self, name):
        sem = self.es.enter_context(self.nc.semaphore(name))
        c = Clock(sem, name)
        self.clks[name] = c
        return c

    def begin(self, engine_name, handle):
        self.cur, self.e = engine_name, handle
        for c in self.clks.values():
            c.n = 0
        self.marks = {}

    def wait(self, eng, clock, val=None):
        v = clock.n if val is None else val
        if v is None or v <= 0:
            return
        key = (eng, clock.name)
        if self.marks.get(key, 0) >= v:
            return
        self.marks[key] = v
        if self.cur == eng:
            self.e.wait_ge(clock.sem, v)

    def op(self, eng, fn, inc=None, inc_by=1, waits=()):
        for wc in waits:
            if isinstance(wc, tuple):
                self.wait(eng, wc[0], wc[1])
            elif wc is not None:
                self.wait(eng, wc)
        tok = None
        if inc is not None:
            inc.n += inc_by
            tok = inc.n
        if self.cur == eng:
            inst = fn(self.e)
            if inc is not None:
                inst.then_inc(inc.sem, inc_by)
        return tok


def build_program():
    nc = bass.Bass("TRN2", target_bir_lowering=False, debug=False)
    es = ExitStack()
    kb = KB(nc, es)

    def din(name, shape):
        return nc.dram_tensor(name, shape, F32, kind="ExternalInput").ap()

    x_fm = din("x_fm", [D, T])
    xbnd_t = din("xbnd_t", [128, 64])
    bflag = din("bflag", [128, 1])
    bsel = din("bsel", [128, 4])
    bin_h = din("bin_h", [128, 37])
    mup_h = din("mup_h", [128, 37])
    convw_h = din("convw_h", [128, 80])
    convb_h = din("convb_h", [128, 20])
    normw_h = din("normw_h", [128, 16])
    bout_h = din("bout_h", [128, 16])
    bprj_h = din("bprj_h", [128, 16])
    a32_h = din("a32_h", [32, 3])
    gwt_h = din("gwt_h", [128, 128])
    W_in = din("W_in", [D, PROJ])
    b_in = din("b_in", [PROJ])
    mup_vec = din("mup_vec", [PROJ])
    conv_w = din("conv_w", [CONV, 4])
    conv_b = din("conv_b", [CONV])
    A_log = din("A_log", [H])
    D_par = din("D_par", [H])
    dt_bias = din("dt_bias", [H])
    norm_w = din("norm_w", [D_SSM])
    W_out = din("W_out", [D_SSM, D])
    b_out = din("b_out", [D])
    WqkvT = din("WqkvT", [D, 3 * D])
    WprojT = din("WprojT", [D, D])
    bproj = din("bproj", [D])
    gate_wT = din("gate_wT", [D, E_MOE])
    w1 = din("w1", [E_MOE, D, I_MOE])
    w3 = din("w3", [E_MOE, D, I_MOE])
    w2 = din("w2", [E_MOE, I_MOE, D])
    sw1 = din("sw1", [D, SH_I])
    sw3 = din("sw3", [D, SH_I])
    sw2 = din("sw2", [SH_I, D])

    out_fm = nc.dram_tensor("out_fm", [D, T], F32, kind="ExternalOutput").ap()

    st_src = nc.dram_tensor("st_src", [128, 512], BF16)
    st_dst = nc.dram_tensor("st_dst", [2, 128, 512], BF16)
    qdram = nc.dram_tensor("qdram", [NH, 128, T], BF16)
    ctx_src = nc.dram_tensor("ctx_src", [NH, 128, T], BF16)
    ctx_dst = nc.dram_tensor("ctx_dst", [4, NH, 128, T], BF16)
    kv_src = nc.dram_tensor("kv_src", [NH, 2, 128, T], BF16)
    kv_dst = nc.dram_tensor("kv_dst", [2, NH, 2, 128, T], BF16)

    SBUF = lambda name, shape, dt: es.enter_context(nc.sbuf_tensor(name, shape, dt))
    PSUM = lambda name, shape, dt: es.enter_context(nc.psum_tensor(name, shape, dt))

    RES = SBUF("RES", [128, FT * T], BF16)
    UB = SBUF("UB", [128, FT * T], BF16)
    YTM = SBUF("YTM", [128, FT * T], BF16)
    XBC = SBUF("XBC", [128, CT * CSTR], BF16)
    WS = SBUF("WS", [128, 13824], BF16)
    LNB = SBUF("LNB", [128, 2 * T], BF16)
    MF = SBUF("MF", [128, 2560], F32)
    MB = SBUF("MB", [128, 5888], BF16)

    bin_t = SBUF("bin_t", [128, 37], F32)
    mup_t = SBUF("mup_t", [128, 37], F32)
    bmup = SBUF("bmup", [128, 37], F32)
    convw_t = SBUF("convw_t", [128, 80], F32)
    convb_t = SBUF("convb_t", [128, 20], F32)
    normw_t = SBUF("normw_t", [128, 16], F32)
    bout_t = SBUF("bout_t", [128, 16], F32)
    bprj_t = SBUF("bprj_t", [128, 16], F32)
    a32 = SBUF("a32", [32, 4], F32)
    dtb2 = SBUF("dtb2", [32, 1], F32)
    flag_t = SBUF("flag_t", [128, 1], F32)
    bsel_t = SBUF("bsel_t", [128, 4], F32)
    dbc = SBUF("dbc", [128, 32], F32)
    row32 = SBUF("row32", [1, 32], F32)
    gwt = SBUF("gwt", [128, FT * E_MOE], BF16)
    UBND = SBUF("UBND", [128, 48], BF16)
    XB3 = SBUF("XB3", [128, 64], BF16)
    SQ3 = SBUF("SQ3", [128, 64], BF16)
    LNB3 = SBUF("LNB3", [128, 8], F32)
    STAT3 = SBUF("STAT3", [1, 32], F32)

    ident = SBUF("ident", [128, 128], BF16)
    onesc = SBUF("onesc", [128, 1], BF16)
    onesrb = SBUF("onesrb", [1, 128], BF16)
    onesf = SBUF("onesf", [128, 128], F32)
    identf = SBUF("identf", [128, 128], F32)
    mask = SBUF("mask", [128, 128], F32)

    SST = SBUF("SST", [128, 512], BF16)
    SSIB = SBUF("SSIB", [128, 512], BF16)
    SROWS = SBUF("SROWS", [32, 2048], F32)
    SCUM = SROWS[:, 0:1024]
    DTT = SROWS[:, 1024:2048]
    EDT = MF[0:32, 1024:2048]
    LA = MF[0:32, 1024:2048]
    EIN = MF[0:32, 1024:2048]
    GCUM = MF[0:32, 0:1024]
    EG = MF[0:32, 0:1024]
    LNBF = LNB[:].bitcast(F32)
    DT_TM = LNBF[:, 0:256]
    S_TM = LNBF[:, 256:512]
    EIN_TM = LNBF[:, 512:768]
    EG_TM = LNBF[:, 768:1024]
    GAM = SBUF("GAM", [128, 256], F32)
    EPSC = SBUF("EPSC", [128, 1], F32)
    MASKB = SBUF("MASKB", [128, 128], BF16)
    NEA_B = SBUF("NEA_B", [128, 32], F32)
    PEA_B = SBUF("PEA_B", [128, 32], F32)
    rowsc = SBUF("rowsc", [1, 32], F32)
    SELG = MB[0:8, 3072:4096]
    RSTDB = MB[0:8, 2048:3072]
    STATR = MF[0:1, 0:2048]
    VARB = MF[0:8, 0:1024]
    RSTD = MF[0:8, 1024:2048]
    GON = SBUF("GON", [128, FT * 8], BF16)
    COEF = SBUF("COEF", [128, 64], F32)
    LGTM = SBUF("LGTM", [128, 64], F32)
    GSC = SBUF("GSC", [128, 96], F32)
    LROW = MF[0:1, 1024:2048]
    LINV = MF[:, 0:1024]
    CC = SBUF("CC", [128, 1], F32)
    MXB = SBUF("MXB", [128, 2], BF16)
    HSB = XBC[:, 12340: 12340 + 2048]
    BOU = XBC[:, 12340: 12340 + 2048]
    QH = XBC[:, 14388: 14388 + 2048]
    KH = XBC[:, 0: 4096]
    VH = XBC[:, 4096: 8192]
    VTM = XBC[:, 8192: 10240]

    PA = PSUM("PA", [128, 512], F32)
    PB = PSUM("PB", [128, 512], F32)
    PC = PSUM("PC", [128, 512], F32)
    PD = PSUM("PD", [128, 512], F32)
    PE1 = PSUM("PE1", [128, 512], F32)
    PF = PSUM("PF", [128, 512], F32)
    PT1 = PSUM("PT1", [128, 1024], BF16)
    PT2 = PSUM("PT2", [128, 1024], BF16)
    PT1f = PT1[:].bitcast(F32)
    PT2f = PT2[:].bitcast(F32)
    MF_b = MF[:].bitcast(BF16)

    nc.const_aps.register(F32, EPS, EPSC[0:1, 0:1])

    ck_t = kb.clk("ck_t")
    ck_v = kb.clk("ck_v")
    ck_s = kb.clk("ck_s")
    ck_g = kb.clk("ck_g")
    wd_clks = [kb.clk(f"ck_wd{i}") for i in range(4)]
    ad_clks = [kb.clk(f"ck_ad{i}") for i in range(4)]
    ck_cc = kb.clk("ck_cc")
    ROT = {"wd": 0, "ad": 0}

    TEN, VEC, SCA, GPS, SYN = "tensor", "vector", "scalar", "gpsimd", "sync"
    ALL_CLKS = (ck_t, ck_v, ck_s, ck_g, ck_cc) + tuple(wd_clks) + tuple(ad_clks)

    def barrier():
        for eng in (TEN, VEC, SCA, GPS, SYN):
            for c in ALL_CLKS:
                kb.wait(eng, c)

    def mm(ps, lhsT, rhs, start, stop, inc=None, waits=(), tp=None):
        return kb.op(TEN, lambda e: e.matmul(ps, lhsT, rhs, start=start, stop=stop,
                                             tile_position=tp),
                     inc=inc, waits=waits)

    def transp(ps, in_ap, inc=None, waits=()):
        pdim = in_ap.shape[0]
        idt = identf if in_ap.dtype == F32 else ident
        return kb.op(TEN, lambda e: e.transpose(ps, in_ap, idt[0:pdim, 0:pdim]),
                     inc=inc, waits=waits)

    def vop(fn, waits=()):
        kb.wait(VEC, ck_v, ck_v.n)
        return kb.op(VEC, fn, inc=ck_v, waits=waits)

    def sop(fn, waits=()):
        kb.wait(SCA, ck_s, ck_s.n)
        return kb.op(SCA, fn, inc=ck_s, waits=waits)

    def gop(fn, waits=()):
        kb.wait(GPS, ck_g, ck_g.n)
        return kb.op(GPS, fn, inc=ck_g, waits=waits)

    def wdma(out_ap, in_ap, waits=(), slow=False):
        kb.wait(GPS, ck_g, ck_g.n)
        c = wd_clks[ROT["wd"] % 4]
        ROT["wd"] += 1
        kb.wait(GPS, c, c.n)
        v = kb.op(GPS, lambda e: e.dma_start(out=out_ap, in_=in_ap,
                                             allow_slow_non_contiguous=slow),
                  inc=c, inc_by=16, waits=waits)
        return (c, v)

    def adma(out_ap, in_ap, waits=()):
        c = ad_clks[ROT["ad"] % 4]
        ROT["ad"] += 1
        kb.wait(SYN, c, c.n)
        v = kb.op(SYN, lambda e: e.dma_start(out=out_ap, in_=in_ap),
                  inc=c, inc_by=16, waits=waits)
        return (c, v)

    def wait_all_dma(eng):
        for c in wd_clks + ad_clks:
            kb.wait(eng, c)

    def setup():
        wdma(RES[:].rearrange("p (f t) -> p f t", f=FT),
             x_fm.rearrange("(f p) t -> p f t", p=128))
        wdma(XB3[:], xbnd_t[:, :])
        wdma(bin_t[:], bin_h[:, :])
        wdma(mup_t[:], mup_h[:, :])
        wdma(convw_t[:], convw_h[:, :])
        wdma(convb_t[:], convb_h[:, :])
        wdma(normw_t[:], normw_h[:, :])
        wdma(bout_t[:], bout_h[:, :])
        wdma(bprj_t[:], bprj_h[:, :])
        wdma(a32[:, 0:3], a32_h[:, :])
        wdma(flag_t[:], bflag[:, :])
        wdma(bsel_t[:], bsel[:, :])
        wdma(row32[:], D_par.rearrange("(o p) -> o p", o=1), slow=True)
        wdma(gwt[:], gwt_h[:, :])
        t_small = [(c, c.n) for c in wd_clks]

        gop(lambda e: e.memset(EPSC[:], EPS))
        gop(lambda e: e.memset(ident[:], 0.0))
        gop(lambda e: e.memset(mask[:], 0.0))
        gop(lambda e: e.memset(onesc[:], 1.0))
        gop(lambda e: e.memset(onesrb[:], 1.0))
        gop(lambda e: e.memset(onesf[:], 1.0))
        gop(lambda e: e.memset(GON[:], 0.0))
        gop(lambda e: e.memset(SST[:], 0.0))
        gop(lambda e: e.affine_select(out=ident[:], in_=ident[:],
                                      compare_op=OP.not_equal, fill=1.0, base=0,
                                      pattern=[[-1, 128]], channel_multiplier=1))
        gop(lambda e: e.memset(identf[:], 0.0))
        gop(lambda e: e.affine_select(out=identf[:], in_=identf[:],
                                      compare_op=OP.not_equal, fill=1.0, base=0,
                                      pattern=[[-1, 128]], channel_multiplier=1))
        gop(lambda e: e.affine_select(out=mask[:], in_=mask[:],
                                      compare_op=OP.is_gt, fill=1.0, base=0,
                                      pattern=[[-1, 128]], channel_multiplier=1))
        gop(lambda e: e.memset(MASKB[:], 0.0))
        gop(lambda e: e.affine_select(out=MASKB[:], in_=MASKB[:],
                                      compare_op=OP.is_gt, fill=1.0, base=0,
                                      pattern=[[-1, 128]], channel_multiplier=1))

        for o in range(FT):
            gop(lambda e, o=o: e.memset(GON[:, o * 8 + o // 2: o * 8 + o // 2 + 1], 1.0))
        for o in range(CT):
            gop(lambda e, o=o: e.memset(XBC[:, o * CSTR: o * CSTR + 4], 0.0))

        vop(lambda e: e.tensor_mul(bmup[:], bin_t[:], mup_t[:]),
            waits=t_small)
        sop(lambda e: e.activation(a32[:, 3:4], a32[:, 0:1], AF.Exp),
            waits=t_small)
        sop(lambda e: e.mul(a32[:, 3:4], a32[:, 3:4], -1.0))
        vop(lambda e: e.tensor_add(dtb2[:], bmup[0:32, 36:37], a32[0:32, 2:3]),
            waits=[(ck_s, ck_s.n)])
        mm(PF[:, 0:32], onesf[0:1, 0:128], row32[:], True, True, inc=ck_t,
           waits=t_small + [(ck_g, ck_g.n)])
        vop(lambda e: e.tensor_copy(dbc[:], PF[:, 0:32]), waits=[(ck_t, ck_t.n)])
        s_nea = ck_s.n
        tnea = transp(PT1f[0:1, 0:32], a32[0:32, 3:4], inc=ck_t, waits=[(ck_s, s_nea)])
        vop(lambda e: e.tensor_copy(rowsc[:], PT1f[0:1, 0:32]), waits=[(ck_t, tnea)])
        v_row = ck_v.n
        mm(PE1[:, 0:32], onesf[0:1, 0:128], rowsc[:], True, True, inc=ck_t,
           waits=[(ck_v, v_row)])
        vop(lambda e: e.tensor_copy(NEA_B[:], PE1[:, 0:32]), waits=[(ck_t, ck_t.n)])
        vop(lambda e: e.tensor_scalar(PEA_B[:], NEA_B[:], -1.0, None, OP.mult))
        barrier()

    def layernorm():
        barrier()
        for f in range(FT):
            for th in range(2):
                bank = PA if th == 0 else PB
                mm(bank[0:1, 0:512], onesc[:],
                   RES[:, f * T + th * 512: f * T + th * 512 + 512],
                   start=(f == 0), stop=(f == FT - 1),
                   inc=ck_t if (f == FT - 1 and th == 1) else None)
        t_sx = ck_t.n
        sq_tick, mmt = {}, {}
        for f in range(FT):
            w = [(ck_t, mmt.get(f - 2))] if f >= 2 else []
            sq_tick[f] = sop(lambda e, f=f: e.activation(
                MB[:, (f % 2) * 1024: (f % 2) * 1024 + 1024],
                RES[:, f * T: f * T + T], AF.Square), waits=w)
            for th in range(2):
                bank = PC if th == 0 else PD
                mmt[f] = mm(bank[0:1, 0:512], onesc[:],
                            MB[:, (f % 2) * 1024 + th * 512: (f % 2) * 1024 + th * 512 + 512],
                            start=(f == 0), stop=(f == FT - 1),
                            inc=ck_t, waits=[(ck_s, sq_tick[f])])
        t_sx2 = ck_t.n
        vop(lambda e: e.tensor_copy(STATR[0:1, 0:512], PA[0:1, 0:512]), waits=[(ck_t, t_sx)])
        vop(lambda e: e.tensor_copy(STATR[0:1, 512:1024], PB[0:1, 0:512]))
        vop(lambda e: e.tensor_copy(STATR[0:1, 1024:1536], PC[0:1, 0:512]),
            waits=[(ck_t, t_sx2)])
        vop(lambda e: e.tensor_copy(STATR[0:1, 1536:2048], PD[0:1, 0:512]))
        vop(lambda e: e.tensor_scalar(STATR[0:1, 0:2048], STATR[0:1, 0:2048],
                                      1.0 / D, None, OP.mult))
        for hh2 in range(2):
            vop(lambda e, hh2=hh2: e.tensor_mul(
                MF[0:1, 2048: 2048 + 512],
                STATR[0:1, hh2 * 512: hh2 * 512 + 512],
                STATR[0:1, hh2 * 512: hh2 * 512 + 512]))
            vop(lambda e, hh2=hh2: e.tensor_sub(
                STATR[0:1, 1024 + hh2 * 512: 1024 + hh2 * 512 + 512],
                STATR[0:1, 1024 + hh2 * 512: 1024 + hh2 * 512 + 512],
                MF[0:1, 2048: 2048 + 512]))
        v2 = ck_v.n
        sop(lambda e: e.activation(STATR[0:1, 1024:2048], STATR[0:1, 1024:2048],
                                   AF.Sqrt, bias=EPS), waits=[(ck_v, v2)])
        s1 = ck_s.n
        vop(lambda e: e.reciprocal(STATR[0:1, 1024:2048], STATR[0:1, 1024:2048]),
            waits=[(ck_s, s1)])
        vop(lambda e: e.tensor_mul(STATR[0:1, 0:1024], STATR[0:1, 0:1024],
                                   STATR[0:1, 1024:2048]))
        vop(lambda e: e.tensor_scalar(STATR[0:1, 0:1024], STATR[0:1, 0:1024],
                                      -1.0, None, OP.mult))
        v4 = ck_v.n
        ev_a = ev_b = None
        for th in range(2):
            mm(PE1[:, 0:512], onesf[0:1, 0:128],
               STATR[0:1, 1024 + th * 512: 1024 + th * 512 + 512], True, True,
               inc=ck_t, waits=[(ck_v, v4), (ck_v, ev_a)])
            ev_a = vop(lambda e, th=th: e.tensor_copy(LNB[:, th * 512: th * 512 + 512],
                                                      PE1[:, 0:512]), waits=[(ck_t, ck_t.n)])
            mm(PF[:, 0:512], onesf[0:1, 0:128],
               STATR[0:1, th * 512: th * 512 + 512], True, True,
               inc=ck_t, waits=[(ck_v, v4), (ck_v, ev_b)])
            ev_b = vop(lambda e, th=th: e.tensor_copy(LNB[:, 1024 + th * 512: 1024 + th * 512 + 512],
                                                      PF[:, 0:512]), waits=[(ck_t, ck_t.n)])
        v_b = ck_v.n
        for f in range(FT):
            vop(lambda e, f=f: e.tensor_mul(
                MB[:, 2048 + (f % 2) * 1024: 2048 + (f % 2) * 1024 + 1024],
                RES[:, f * T: f * T + T], LNB[:, 0:1024]), waits=[(ck_v, v_b)])
            vop(lambda e, f=f: e.tensor_add(
                UB[:, f * T: f * T + T],
                MB[:, 2048 + (f % 2) * 1024: 2048 + (f % 2) * 1024 + 1024],
                LNB[:, 1024:2048]))
        barrier()

    def mamba_boundary_ln():
        sop(lambda e: e.activation(SQ3[:], XB3[:], AF.Square))
        s_sq3 = ck_s.n
        for f in range(FT):
            mm(PE1[0:1, 0:4], onesc[:], XB3[:, f * 4: f * 4 + 4],
               start=(f == 0), stop=(f == FT - 1),
               inc=ck_t if f == FT - 1 else None)
        for f in range(FT):
            mm(PF[0:1, 0:4], onesc[:], SQ3[:, f * 4: f * 4 + 4],
               start=(f == 0), stop=(f == FT - 1),
               inc=ck_t if f == FT - 1 else None, waits=[(ck_s, s_sq3)])
        t_b = ck_t.n
        vop(lambda e: e.tensor_copy(STAT3[0:1, 0:4], PE1[0:1, 0:4]), waits=[(ck_t, t_b)])
        vop(lambda e: e.tensor_copy(STAT3[0:1, 4:8], PF[0:1, 0:4]))
        vop(lambda e: e.tensor_scalar(STAT3[0:1, 8:16], STAT3[0:1, 0:8],
                                      1.0 / D, None, OP.mult))
        vop(lambda e: e.tensor_mul(STAT3[0:1, 16:20], STAT3[0:1, 8:12], STAT3[0:1, 8:12]))
        vop(lambda e: e.tensor_sub(STAT3[0:1, 12:16], STAT3[0:1, 12:16], STAT3[0:1, 16:20]))
        v2 = ck_v.n
        sop(lambda e: e.activation(STAT3[0:1, 12:16], STAT3[0:1, 12:16], AF.Sqrt,
                                   bias=EPS), waits=[(ck_v, v2)])
        s2 = ck_s.n
        vop(lambda e: e.reciprocal(STAT3[0:1, 12:16], STAT3[0:1, 12:16]),
            waits=[(ck_s, s2)])
        vop(lambda e: e.tensor_mul(STAT3[0:1, 8:12], STAT3[0:1, 8:12], STAT3[0:1, 12:16]))
        vop(lambda e: e.tensor_scalar(STAT3[0:1, 8:12], STAT3[0:1, 8:12],
                                      -1.0, None, OP.mult))
        v4 = ck_v.n
        mm(PE1[:, 0:4], onesf[0:1, 0:128], STAT3[0:1, 12:16], True, True, inc=ck_t,
           waits=[(ck_v, v4)])
        vop(lambda e: e.tensor_copy(LNB3[:, 0:4], PE1[:, 0:4]), waits=[(ck_t, ck_t.n)])
        mm(PF[:, 0:4], onesf[0:1, 0:128], STAT3[0:1, 8:12], True, True, inc=ck_t,
           waits=[(ck_v, ck_v.n)])
        vop(lambda e: e.tensor_copy(LNB3[:, 4:8], PF[:, 0:4]), waits=[(ck_t, ck_t.n)])
        for f in range(FT):
            vop(lambda e, f=f: e.tensor_mul(MF[:, 1024 + f * 3: 1024 + f * 3 + 3],
                                            XB3[:, f * 4: f * 4 + 3], LNB3[:, 0:3]))
            vop(lambda e, f=f: e.tensor_add(UBND[:, f * 3: f * 3 + 3],
                                            MF[:, 1024 + f * 3: 1024 + f * 3 + 3],
                                            LNB3[:, 4:7]))
        return ck_v.n

    def mamba_inproj(v_ubnd):
        banks = (PA, PB, PC, PD)
        blk = FT * 128
        mm_done, epi_done, bnd_s, bnd_v = {}, {}, {}, {}
        for oi in range(37):
            ow = 128 if oi < 36 else 32
            bsl = WS[:, (oi % 2) * blk: (oi % 2) * blk + FT * ow]
            w = [(ck_t, mm_done.get(oi - 2))] if oi >= 2 else []
            dtok = wdma(bsl.rearrange("p (kt o) -> p kt o", kt=FT),
                        W_in[:, oi * 128: oi * 128 + ow]
                        .rearrange("(kt p) o -> p kt o", p=128), waits=w)
            tk = None
            for th in range(2):
                bank = banks[(oi * 2 + th) % 4]
                kb.wait(TEN, dtok[0], dtok[1])
                if oi * 2 + th >= 4:
                    kb.wait(TEN, ck_s, epi_done.get(oi * 2 + th - 4))
                for kt in range(FT):
                    tk = mm(bank[0:ow, 0:512], bsl[:, kt * ow: kt * ow + ow],
                            UB[:, kt * T + th * 512: kt * T + th * 512 + 512],
                            start=(kt == 0), stop=(kt == FT - 1),
                            inc=ck_t if kt == FT - 1 else None)
                if oi < 16:
                    epi_done[oi * 2 + th] = sop(
                        lambda e, oi=oi, th=th, bank=bank: e.activation(
                            YTM[:, oi * T + th * 512: oi * T + th * 512 + 512],
                            bank[:, 0:512], AF.Silu,
                            bias=bmup[:, oi: oi + 1], scale=mup_t[:, oi: oi + 1]),
                        waits=[(ck_t, tk)])
                elif oi < 36:
                    o = oi - 16
                    epi_done[oi * 2 + th] = sop(
                        lambda e, o=o, oi=oi, th=th, bank=bank: e.activation(
                            XBC[:, o * CSTR + 4 + th * 512: o * CSTR + 4 + th * 512 + 512],
                            bank[:, 0:512], AF.Identity,
                            bias=bmup[:, oi: oi + 1], scale=mup_t[:, oi: oi + 1]),
                        waits=[(ck_t, tk)])
                else:
                    sop(lambda e, th=th, bank=bank: e.activation(
                        EDT[:, th * 512: th * 512 + 512], bank[0:32, 0:512],
                        AF.Exp, bias=dtb2[:, 0:1], scale=mup_t[0:32, 36:37]),
                        waits=[(ck_t, tk)])
                    epi_done[oi * 2 + th] = sop(
                        lambda e, th=th: e.activation(
                            DTT[:, th * 512: th * 512 + 512],
                            EDT[:, th * 512: th * 512 + 512], AF.Ln, bias=1.0))
            mm_done[oi] = tk
            if 16 <= oi < 36:
                o = oi - 16
                kb.wait(TEN, ck_v, v_ubnd)
                kb.wait(TEN, ck_s, bnd_s.get(o - 1))
                tkb = None
                for kt in range(FT):
                    tkb = mm(PE1[:, 0:3], bsl[:, kt * 128: kt * 128 + 128],
                             UBND[:, kt * 3: kt * 3 + 3],
                             start=(kt == 0), stop=(kt == FT - 1),
                             inc=ck_t if kt == FT - 1 else None)
                mm_done[oi] = tkb
                bnd_s[o] = sop(lambda e, o=o, oi=oi: e.activation(
                    MF[:, 2048 + (o % 2) * 4: 2048 + (o % 2) * 4 + 3], PE1[:, 0:3],
                    AF.Identity, bias=bmup[:, oi: oi + 1], scale=mup_t[:, oi: oi + 1]),
                    waits=[(ck_t, tkb), (ck_v, bnd_v.get(o - 2))])
                bnd_v[o] = vop(lambda e, o=o: e.tensor_scalar(
                    XBC[:, o * CSTR + 1: o * CSTR + 4],
                    MF[:, 2048 + (o % 2) * 4: 2048 + (o % 2) * 4 + 3],
                    flag_t[:, 0:1], None, OP.mult), waits=[(ck_s, bnd_s[o])])
        barrier()

    def mamba_conv():
        for o in range(CT):
            base = o * CSTR
            vop(lambda e, base=base, o=o: e.tensor_scalar(
                MF[:, 0:1024], XBC[:, base + 1: base + 1 + 1024],
                convw_t[:, o * 4: o * 4 + 1], None, OP.mult))
            for k in range(1, 4):
                vop(lambda e, base=base, o=o, k=k: e.scalar_tensor_tensor(
                    MF[:, 0:1024], XBC[:, base + 1 + k: base + 1 + k + 1024],
                    convw_t[:, o * 4 + k: o * 4 + k + 1], MF[:, 0:1024],
                    OP.mult, OP.add))
            v_c = ck_v.n
            sconv = sop(lambda e, base=base, o=o: e.activation(
                XBC[:, base + 4: base + 4 + 1024], MF[:, 0:1024], AF.Silu,
                bias=convb_t[:, o: o + 1]), waits=[(ck_v, v_c)])
            kb.wait(VEC, ck_s, sconv)
        barrier()

    def mamba_rows():
        sop(lambda e: e.mul(LA, DTT, a32[:, 3:4]))
        s_la = ck_s.n
        for c in range(NCH):
            vop(lambda e, c=c: e.tensor_tensor_scan(
                SCUM[:, c * Q: c * Q + Q], onesf[0:32, 0:128],
                LA[:, c * Q: c * Q + Q], 0.0, OP.mult, OP.add),
                waits=[(ck_s, s_la)])
        for c in range(NCH):
            init = 0.0 if c == 0 else GCUM[:, c * Q - 1: c * Q]
            vop(lambda e, c=c, init=init: e.tensor_tensor_scan(
                GCUM[:, c * Q: c * Q + Q], onesf[0:32, 0:128],
                LA[:, c * Q: c * Q + Q], init, OP.mult, OP.add))
        v_sc = ck_v.n
        sop(lambda e: e.activation(EIN, SCUM, AF.Exp), waits=[(ck_v, v_sc)])
        sop(lambda e: e.activation(EG, GCUM, AF.Exp))
        s_e = ck_s.n
        vop(lambda e: e.tensor_scalar(EG, EG, flag_t[0:32, 0:1], None, OP.mult),
            waits=[(ck_s, s_e)])
        v_eg = ck_v.n
        tr_ev = {}
        i = 0
        for (srcT, dstT) in ((DTT, DT_TM), (EIN, EIN_TM), (EG, EG_TM)):
            for c in range(NCH):
                pf = PT1f if i % 2 == 0 else PT2f
                w = [(ck_v, v_eg), (ck_s, s_e)]
                if i >= 2:
                    w.append((ck_v, tr_ev[i - 2]))
                tt = transp(pf[0:128, 0:32], srcT[0:32, c * Q: c * Q + Q], inc=ck_t,
                            waits=w)
                tr_ev[i] = vop(lambda e, pf=pf, dstT=dstT, c=c: e.tensor_copy(
                    dstT[:, c * 32: c * 32 + 32], pf[0:128, 0:32]), waits=[(ck_t, tt)])
                i += 1
        barrier()

    XTM0, YTMB0, ATB0, XT0, XD0, BTM0 = 0, 2048, 4096, 5120, 5376, 5632

    def scan_chunk(c, wave_fin, tr_state, g_ev):
        kb.wait(TEN, ck_s, ck_s.n)
        for wv in range(2):
            pt = PT1 if wv == 0 else PT2
            w = [(ck_v, tr_state.get(("pt", wv)))]
            tk = None
            for i in range(8):
                f = wv * 8 + i
                tk = transp(pt[:, i * 128: i * 128 + 128],
                            XBC[:, f * CSTR + 4 + c * Q: f * CSTR + 4 + c * Q + Q],
                            inc=ck_t if i == 7 else None, waits=w if i == 0 else ())
            ev = vop(lambda e, pt=pt, wv=wv: e.tensor_copy(
                MB[:, XTM0 + wv * 1024: XTM0 + wv * 1024 + 1024], pt[:]),
                waits=[(ck_t, tk)])
            tr_state[("pt", wv)] = ev
        v_xtm = ck_v.n
        if SCANPART <= 1:
            return
        tk = None
        for i in range(2):
            w = [(ck_v, tr_state[("pt", 0)])] if i == 0 else ()
            tk = transp(PT1[:, i * 128: i * 128 + 128],
                        XBC[:, (16 + i) * CSTR + 4 + c * Q: (16 + i) * CSTR + 4 + c * Q + Q],
                        inc=ck_t if i == 1 else None, waits=w)
        tr_state[("pt", 0)] = vop(lambda e: e.tensor_copy(
            MB[:, BTM0: BTM0 + 256], PT1[:, 0:256]), waits=[(ck_t, tk)])
        if SCANPART <= 2:
            return
        gbanks = [PA, PB, PC, PD, PE1, PF, PT1f, PT2f]
        kb.wait(TEN, ck_v, wave_fin.get(c * 8 - 1))
        kb.wait(TEN, ck_v, tr_state.get(("pt", 0)))
        kb.wait(TEN, ck_v, tr_state.get(("pt", 1)))
        tkg = None
        for g in range(8):
            qt, q4 = g // 4, 32 * (g % 4)
            tkg = mm(gbanks[g][:, 0:128],
                     XBC[q4: q4 + 32, (16 + qt) * CSTR + 4 + c * Q:
                         (16 + qt) * CSTR + 4 + c * Q + Q],
                     XBC[q4: q4 + 32, (18 + qt) * CSTR + 4 + c * Q:
                         (18 + qt) * CSTR + 4 + c * Q + Q],
                     start=True, stop=True, tp=(q4, 0) if q4 == 96 else None,
                     inc=ck_t if g == 7 else None)
        for g in range(8):
            g_ev[0] = vop(lambda e, g=g: e.tensor_mul(
                MF[:, g * 128: g * 128 + 128],
                gbanks[g][:, 0:128], mask[:]),
                waits=[(ck_t, tkg)])
        v_mg = ck_v.n
        if SCANPART <= 3:
            return

        for wv in range(8):
            aw = c * 8 + wv
            bcb = PE1 if wv % 2 == 0 else PF
            y1b = PA if wv % 2 == 0 else PC
            y2b = PB if wv % 2 == 0 else PD
            hs = [wv * 4 + i for i in range(4)]
            kb.wait(TEN, ck_v, v_mg)
            kb.wait(TEN, ck_v, wave_fin.get(aw - 2))
            for i, h in enumerate(hs):
                vop(lambda e, i=i, h=h, c=c: e.tensor_scalar(
                    MB[:, ATB0 + i * 128: ATB0 + i * 128 + 128], onesf[:],
                    DT_TM[:, c * 32 + h: c * 32 + h + 1],
                    NEA_B[:, h: h + 1], OP.mult, OP.mult))
                vop(lambda e, i=i, h=h, c=c: e.tensor_scalar(
                    MF_b[:, 3072 + i * 128: 3072 + i * 128 + 128], onesf[:],
                    DT_TM[:, c * 32 + h: c * 32 + h + 1],
                    PEA_B[:, h: h + 1], OP.mult, OP.mult))
            v_lab = ck_v.n
            kb.wait(TEN, ck_v, v_lab)
            t_bc = None
            for i, h in enumerate(hs):
                mm(bcb[:, i * 128: i * 128 + 128],
                   MB[:, ATB0 + i * 128: ATB0 + i * 128 + 128], MASKB[:],
                   True, False)
                mm(bcb[:, i * 128: i * 128 + 128], MASKB[:],
                   MF_b[:, 3072 + i * 128: 3072 + i * 128 + 128],
                   False, True)
                t_bc = mm(PT1f[:, wv * 4 + i: wv * 4 + i + 1],
                          MB[:, ATB0 + i * 128: ATB0 + i * 128 + 128], onesc[:],
                          True, True, inc=ck_t if i == 3 else None)
            for i, h in enumerate(hs):
                vop(lambda e, bcb=bcb, i=i: e.tensor_scalar(
                    MF[:, 1024 + i * 128: 1024 + i * 128 + 128],
                    bcb[:, i * 128: i * 128 + 128], 0.0, None, OP.min),
                    waits=[(ck_t, t_bc)])
            v_dif = ck_v.n
            if SCANPART <= 4:
                wave_fin[aw] = ck_v.n
                continue
            for i, h in enumerate(hs):
                sop(lambda e, i=i, h=h, c=c, wv=wv: e.activation(
                    GAM[:, c * 32 + h: c * 32 + h + 1],
                    PT1f[:, wv * 4 + i: wv * 4 + i + 1], AF.Exp),
                    waits=[(ck_t, t_bc)])
            for i in range(4):
                sop(lambda e, i=i: e.activation(
                    MF[:, 1536 + i * 128: 1536 + i * 128 + 128],
                    MF[:, 1024 + i * 128: 1024 + i * 128 + 128], AF.Exp),
                    waits=[(ck_v, v_dif)])
            s_ex = ck_s.n
            for i, h in enumerate(hs):
                gg = h // 4
                vop(lambda e, i=i, gg=gg: e.tensor_mul(
                    MB[:, ATB0 + i * 128: ATB0 + i * 128 + 128],
                    MF[:, gg * 128: gg * 128 + 128],
                    MF[:, 1536 + i * 128: 1536 + i * 128 + 128]),
                    waits=[(ck_s, s_ex)])
                vop(lambda e, i=i, h=h, c=c: e.tensor_scalar(
                    MB[:, XT0 + i * 64: XT0 + i * 64 + 64],
                    MB[:, XTM0 + h * 64: XTM0 + h * 64 + 64],
                    DT_TM[:, c * 32 + h: c * 32 + h + 1], None, OP.mult))
                vop(lambda e, i=i: e.tensor_scalar(
                    MB[:, XD0 + i * 64: XD0 + i * 64 + 64],
                    MB[:, XT0 + i * 64: XT0 + i * 64 + 64],
                    MF[:, 1536 + i * 128 + 127: 1536 + i * 128 + 128],
                    None, OP.mult))
            v_at = ck_v.n
            if SCANPART <= 5:
                wave_fin[aw] = ck_v.n
                continue
            kb.wait(TEN, ck_v, v_at)
            t_y = None
            for i, h in enumerate(hs):
                gg = h // 4
                qt, q4 = gg // 4, 32 * (gg % 4)
                mm(y1b[:, i * 64: i * 64 + 64],
                   MB[:, ATB0 + i * 128: ATB0 + i * 128 + 128],
                   MB[:, XT0 + i * 64: XT0 + i * 64 + 64], True, True)
                mm(y2b[:, i * 64: i * 64 + 64],
                   XBC[q4: q4 + 32, (18 + qt) * CSTR + 4 + c * Q:
                       (18 + qt) * CSTR + 4 + c * Q + Q],
                   SST[q4: q4 + 32, qt * 256 + (h % 4) * 64: qt * 256 + (h % 4) * 64 + 64],
                   True, True, tp=(q4, 0) if q4 == 96 else None)
                t_y = mm(y1b[q4: q4 + 32, 256 + i * 64: 256 + i * 64 + 64],
                         MB[:, BTM0 + gg * 32: BTM0 + gg * 32 + 32],
                         MB[:, XD0 + i * 64: XD0 + i * 64 + 64],
                         True, True, tp=(0, q4),
                         inc=ck_t if i == 3 else None)
            for i, h in enumerate(hs):
                gg = h // 4
                qt, q4 = gg // 4, 32 * (gg % 4)
                vop(lambda e, y2b=y2b, i=i, h=h, c=c: e.tensor_scalar(
                    MF[:, 2048 + i * 64: 2048 + i * 64 + 64],
                    y2b[:, i * 64: i * 64 + 64],
                    EIN_TM[:, c * 32 + h: c * 32 + h + 1], None, OP.mult),
                    waits=[(ck_t, t_y)])
                vop(lambda e, y1b=y1b, i=i, h=h: e.scalar_tensor_tensor(
                    MF[:, 2304 + i * 64: 2304 + i * 64 + 64],
                    MB[:, XTM0 + h * 64: XTM0 + h * 64 + 64],
                    dbc[:, h: h + 1], y1b[:, i * 64: i * 64 + 64],
                    OP.mult, OP.add))
                vop(lambda e, i=i, h=h: e.tensor_add(
                    MB[:, YTMB0 + h * 64: YTMB0 + h * 64 + 64],
                    MF[:, 2048 + i * 64: 2048 + i * 64 + 64],
                    MF[:, 2304 + i * 64: 2304 + i * 64 + 64]))
                vop(lambda e, y1b=y1b, i=i, h=h, qt=qt, q4=q4, c=c: e.scalar_tensor_tensor(
                    SST[q4: q4 + 32, qt * 256 + (h % 4) * 64: qt * 256 + (h % 4) * 64 + 64],
                    SST[q4: q4 + 32, qt * 256 + (h % 4) * 64: qt * 256 + (h % 4) * 64 + 64],
                    GAM[q4: q4 + 32, c * 32 + h: c * 32 + h + 1],
                    y1b[q4: q4 + 32, 256 + i * 64: 256 + i * 64 + 64],
                    OP.mult, OP.add))
            wave_fin[aw] = ck_v.n
        if SCANPART <= 6:
            return
        v_y = ck_v.n
        for wv in range(2):
            pt = PT1 if wv == 0 else PT2
            w = [(ck_v, tr_state[("pt", wv)]), (ck_v, v_y)]
            tk = None
            for i in range(8):
                f = wv * 8 + i
                tk = transp(pt[:, i * 128: i * 128 + 128],
                            MB[:, YTMB0 + f * 128: YTMB0 + f * 128 + 128],
                            inc=ck_t if i == 7 else None, waits=w if i == 0 else ())
            last = None
            for i in range(8):
                f = wv * 8 + i
                last = vop(lambda e, pt=pt, i=i, f=f, c=c: e.tensor_copy(
                    UB[:, f * T + c * Q: f * T + c * Q + Q],
                    pt[:, i * 128: i * 128 + 128]), waits=[(ck_t, tk)])
            tr_state[("pt", wv)] = last

    def mamba_state_exchange():
        barrier()
        dst_tok = adma(st_src.ap(), SST[:])
        kb.op(GPS, lambda e: e.collective_compute(
            "AllGather", OP.bypass, replica_groups=PAIR_GROUPS,
            ins=[st_src.ap()], outs=[st_dst.ap()]),
            inc=ck_cc, waits=[dst_tok, (ck_g, ck_g.n)])
        adma(SSIB[:], st_dst[0], waits=[(ck_cc, ck_cc.n)])
        barrier()

    def mamba_sib_correction():
        corr_tr = {}
        for c in range(NCH):
            t_y = None
            vfirst = {}
            for h in range(H):
                gg = h // 4
                qt, q4 = gg // 4, 32 * (gg % 4)
                bank = PA if h % 2 == 0 else PB
                kb.wait(TEN, ck_v, vfirst.get(h - 2))
                if h < 2:
                    kb.wait(TEN, ck_v, corr_tr.get("last", None))
                t_y = mm(bank[:, 0:64],
                         XBC[q4: q4 + 32, (18 + qt) * CSTR + 4 + c * Q:
                             (18 + qt) * CSTR + 4 + c * Q + Q],
                         SSIB[q4: q4 + 32, qt * 256 + (h % 4) * 64:
                              qt * 256 + (h % 4) * 64 + 64],
                         True, True, tp=(q4, 0) if q4 == 96 else None, inc=ck_t)
                vfirst[h] = vop(lambda e, bank=bank, h=h, c=c: e.tensor_scalar(
                    MB[:, YTMB0 + h * 64: YTMB0 + h * 64 + 64],
                    bank[:, 0:64], EG_TM[:, c * 32 + h: c * 32 + h + 1],
                    None, OP.mult), waits=[(ck_t, t_y)])
            v_fin = ck_v.n
            for wv in range(2):
                pt = PT1 if wv == 0 else PT2
                w = [(ck_v, corr_tr.get(wv)), (ck_v, v_fin)]
                tk = None
                for i in range(8):
                    f = wv * 8 + i
                    tk = transp(pt[:, i * 128: i * 128 + 128],
                                MB[:, YTMB0 + f * 128: YTMB0 + f * 128 + 128],
                                inc=ck_t if i == 7 else None, waits=w if i == 0 else ())
                last = None
                for i in range(8):
                    f = wv * 8 + i
                    last = vop(lambda e, pt=pt, i=i, f=f, c=c: e.tensor_add(
                        UB[:, f * T + c * Q: f * T + c * Q + Q],
                        UB[:, f * T + c * Q: f * T + c * Q + Q],
                        pt[:, i * 128: i * 128 + 128]), waits=[(ck_t, tk)])
                corr_tr[wv] = last
            corr_tr["last"] = ck_v.n
        barrier()

    def mamba_gate_norm_outproj():
        for f in range(FT):
            vop(lambda e, f=f: e.tensor_mul(UB[:, f * T: f * T + T],
                                            UB[:, f * T: f * T + T],
                                            YTM[:, f * T: f * T + T]))
        v_g = ck_v.n
        sq_s, mmt = {}, {}
        for f in range(FT):
            w = [(ck_v, v_g)]
            if f >= 2:
                w.append((ck_t, mmt.get(f - 2)))
            sq_s[f] = sop(lambda e, f=f: e.activation(
                MB[:, (f % 2) * 1024: (f % 2) * 1024 + 1024],
                UB[:, f * T: f * T + T], AF.Square), waits=w)
            for th in range(2):
                bank = PE1 if th == 0 else PF
                mmt[f] = mm(bank[0:8, 0:512], GON[:, f * 8: f * 8 + 8],
                            MB[:, (f % 2) * 1024 + th * 512: (f % 2) * 1024 + th * 512 + 512],
                            start=(f == 0), stop=(f == FT - 1),
                            inc=ck_t, waits=[(ck_s, sq_s[f])])
        t_var = ck_t.n
        vop(lambda e: e.tensor_copy(VARB[:, 0:512], PE1[0:8, 0:512]), waits=[(ck_t, t_var)])
        vop(lambda e: e.tensor_copy(VARB[:, 512:1024], PF[0:8, 0:512]))
        vop(lambda e: e.tensor_scalar(RSTD[:], VARB[:], 1.0 / 256.0, EPS, OP.mult, OP.add))
        v_r = ck_v.n
        sop(lambda e: e.activation(RSTD[:], RSTD[:], AF.Sqrt), waits=[(ck_v, v_r)])
        s_r = ck_s.n
        vop(lambda e: e.reciprocal(RSTD[:], RSTD[:]), waits=[(ck_s, s_r)])
        vop(lambda e: e.tensor_copy(RSTDB, RSTD))
        v_rr = ck_v.n
        gop(lambda e: e.memset(SELG, 0.0), waits=[(ck_t, t_var)])
        gop(lambda e: e.affine_select(
            out=SELG.rearrange("p (g m) -> p g m", g=8),
            in_=SELG.rearrange("p (g m) -> p g m", g=8),
            compare_op=OP.not_equal, fill=1.0, base=0,
            pattern=[[-1, 8], [0, 128]], channel_multiplier=1))
        g_sel = ck_g.n
        ap_ev = {}
        for g in range(8):
            for th in range(2):
                bank = PE1 if (g * 2 + th) % 2 == 0 else PF
                kb.wait(TEN, ck_v, v_rr)
                kb.wait(TEN, ck_v, ap_ev.get(g * 2 + th - 2))
                kb.wait(TEN, ck_g, g_sel)
                tkb = mm(bank[:, 0:512], SELG[:, g * 128: g * 128 + 128],
                         RSTDB[:, th * 512: th * 512 + 512], True, True,
                         inc=ck_t)
                last = None
                for sub in range(2):
                    f = 2 * g + sub
                    last = vop(lambda e, f=f, th=th, bank=bank: e.scalar_tensor_tensor(
                        UB[:, f * T + th * 512: f * T + th * 512 + 512],
                        UB[:, f * T + th * 512: f * T + th * 512 + 512],
                        normw_t[:, f: f + 1], bank[:, 0:512],
                        OP.mult, OP.mult), waits=[(ck_t, tkb)])
                ap_ev[g * 2 + th] = last
        barrier()
        blk = FT * 128
        mm_done, epi_done = {}, {}
        for oi in range(FT):
            bsl = WS[:, (oi % 2) * blk: (oi % 2) * blk + blk]
            w = [(ck_t, mm_done.get(oi - 2))] if oi >= 2 else []
            dtok = wdma(bsl.rearrange("p (kt o) -> p kt o", kt=FT),
                        W_out[:, oi * 128: oi * 128 + 128]
                        .rearrange("(kt p) o -> p kt o", p=128), waits=w)
            tk = None
            for th in range(2):
                bank = (PA, PB, PC, PD)[(oi * 2 + th) % 4]
                kb.wait(TEN, dtok[0], dtok[1])
                if oi * 2 + th >= 4:
                    kb.wait(TEN, ck_v, epi_done.get(oi * 2 + th - 4))
                for kt in range(FT):
                    tk = mm(bank[:, 0:512], bsl[:, kt * 128: kt * 128 + 128],
                            UB[:, kt * T + th * 512: kt * T + th * 512 + 512],
                            start=(kt == 0), stop=(kt == FT - 1),
                            inc=ck_t if kt == FT - 1 else None)
                epi_done[oi * 2 + th] = vop(
                    lambda e, oi=oi, th=th, bank=bank: e.scalar_tensor_tensor(
                        RES[:, oi * T + th * 512: oi * T + th * 512 + 512],
                        bank[:, 0:512], bout_t[:, oi: oi + 1],
                        RES[:, oi * T + th * 512: oi * T + th * 512 + 512],
                        OP.add, OP.add), waits=[(ck_t, tk)])
            mm_done[oi] = tk
        barrier()

    def mamba():
        v_ubnd = mamba_boundary_ln()
        mamba_inproj(v_ubnd)
        if STAGE <= 1:
            barrier()
            return
        mamba_conv()
        if STAGE <= 2:
            barrier()
            return
        mamba_rows()
        if STAGE <= 3:
            barrier()
            return
        wave_fin, tr_state, g_ev = {}, {}, {}
        for c in range(min(NCH, STAGE - 3)):
            scan_chunk(c, wave_fin, tr_state, g_ev)
        if SCAN_ONLY or STAGE < 99:
            barrier()
            return
        mamba_state_exchange()
        mamba_sib_correction()
        mamba_gate_norm_outproj()

    def moe_gate():
        for th in range(2):
            bank = PE1 if th == 0 else PF
            tk = None
            for kt in range(FT):
                tk = mm(bank[0:8, 0:512], gwt[:, kt * 8: kt * 8 + 8],
                        UB[:, kt * T + th * 512: kt * T + th * 512 + 512],
                        start=(kt == 0), stop=(kt == FT - 1),
                        inc=ck_t if kt == FT - 1 else None)
        t_lg = ck_t.n
        vop(lambda e: e.tensor_copy(MF[0:8, 0:512], PE1[0:8, 0:512]), waits=[(ck_t, t_lg)])
        vop(lambda e: e.tensor_copy(MF[0:8, 512:1024], PF[0:8, 0:512]))
        v_lg = ck_v.n
        ltr = {}
        for tt in range(8):
            pf = PT1f if tt % 2 == 0 else PT2f
            w = [(ck_v, v_lg)]
            if tt >= 2:
                w.append((ck_v, ltr[tt - 2]))
            tk = transp(pf[0:128, 0:8], MF[0:8, tt * 128: tt * 128 + 128],
                        inc=ck_t, waits=w)
            ltr[tt] = vop(lambda e, pf=pf, tt=tt: e.tensor_copy(
                LGTM[:, tt * 8: tt * 8 + 8], pf[0:128, 0:8]), waits=[(ck_t, tk)])
        for tt in range(8):
            base = tt * 8
            vop(lambda e, base=base: e.reduce_max(GSC[:, 64:65], LGTM[:, base: base + 8],
                                                  axis=AX.X, negate=True))
            v_m = ck_v.n
            sop(lambda e, base=base: e.activation(GSC[:, 68:76], LGTM[:, base: base + 8],
                                                  AF.Exp, bias=GSC[:, 64:65]),
                waits=[(ck_v, v_m)])
            s_x = ck_s.n
            vop(lambda e: e.reduce_max(GSC[:, 65:66], GSC[:, 68:76], axis=AX.X),
                waits=[(ck_s, s_x)])
            vop(lambda e: e.tensor_scalar(GSC[:, 76:84], GSC[:, 68:76],
                                          GSC[:, 65:66], None, OP.is_equal))
            vop(lambda e: e.scalar_tensor_tensor(GSC[:, 76:84], GSC[:, 76:84],
                                                 -1e30, GSC[:, 68:76], OP.mult, OP.add))
            vop(lambda e: e.reduce_max(GSC[:, 66:67], GSC[:, 76:84], axis=AX.X))
            vop(lambda e: e.tensor_scalar(GSC[:, 76:84], GSC[:, 68:76],
                                          GSC[:, 66:67], None, OP.is_ge))
            vop(lambda e: e.tensor_mul(GSC[:, 76:84], GSC[:, 76:84], GSC[:, 68:76]))
            vop(lambda e: e.reduce_sum(GSC[:, 67:68], GSC[:, 76:84], axis=AX.X))
            vop(lambda e: e.reciprocal(GSC[:, 67:68], GSC[:, 67:68]))
            vop(lambda e, tt=tt: e.tensor_scalar(COEF[:, tt * 8: tt * 8 + 8],
                                                 GSC[:, 76:84], GSC[:, 67:68],
                                                 None, OP.mult))
        barrier()

    def moe_mlp(e_i, srcw1, srcw3, srcw2, i0, nI, accum_mode, st):
        blkc = FT * 128
        for it in range(nI):
            git = st["git"]
            st["git"] += 1
            b1 = WS[:, (git % 2) * blkc: (git % 2) * blkc + blkc]
            b3 = WS[:, 4096 + (git % 2) * blkc: 4096 + (git % 2) * blkc + blkc]
            w = [(ck_t, st["hg_mm"].get(git - 2))]
            d1 = wdma(b1.rearrange("p (kt o) -> p kt o", kt=FT),
                      srcw1[:, (i0 + it) * 128: (i0 + it) * 128 + 128]
                      .rearrange("(kt p) o -> p kt o", p=128), waits=w)
            d3 = wdma(b3.rearrange("p (kt o) -> p kt o", kt=FT),
                      srcw3[:, (i0 + it) * 128: (i0 + it) * 128 + 128]
                      .rearrange("(kt p) o -> p kt o", p=128))
            kb.wait(TEN, d1[0], d1[1])
            kb.wait(TEN, d3[0], d3[1])
            kb.wait(TEN, ck_v, st["gt_v"].get(git - 2))
            bset = ((PA, PB, PC, PD), (PE1, PF, PT1f, PT2f))[git % 2]
            tkh = tkg = None
            for th in range(2):
                hb = bset[th]
                gb = bset[2 + th]
                for kt in range(FT):
                    tkh = mm(hb[:, 0:512], b1[:, kt * 128: kt * 128 + 128],
                             UB[:, kt * T + th * 512: kt * T + th * 512 + 512],
                             start=(kt == 0), stop=(kt == FT - 1),
                             inc=ck_t if kt == FT - 1 else None)
                for kt in range(FT):
                    tkg = mm(gb[:, 0:512], b3[:, kt * 128: kt * 128 + 128],
                             UB[:, kt * T + th * 512: kt * T + th * 512 + 512],
                             start=(kt == 0), stop=(kt == FT - 1),
                             inc=ck_t if kt == FT - 1 else None)
                ss = sop(lambda e, th=th, hb=hb, git=git: e.activation(
                    HSB[:, (git % 2) * 1024 + th * 512: (git % 2) * 1024 + th * 512 + 512],
                    hb[:, 0:512], AF.Silu), waits=[(ck_t, tkh)])
                st["gt_v"][git] = vop(lambda e, th=th, gb=gb, it=it, git=git: e.tensor_mul(
                    XBC[:, it * CSTR + th * 512: it * CSTR + th * 512 + 512],
                    HSB[:, (git % 2) * 1024 + th * 512: (git % 2) * 1024 + th * 512 + 512],
                    gb[:, 0:512]), waits=[(ck_s, ss), (ck_t, tkg)])
            st["hg_mm"][git] = tkg
        v_gt = ck_v.n
        for oc in range(8):
            goc = st["goc"]
            st["goc"] += 1
            wb = WS[:, 8192 + (goc % 2) * 2816: 8192 + (goc % 2) * 2816 + 2816]
            w = [(ck_t, st["y_mm"].get(goc - 2))]
            dtok = wdma(wb.rearrange("p (it o) -> p it o", it=nI),
                        srcw2[i0 * 128: i0 * 128 + nI * 128, oc * 256: oc * 256 + 256]
                        .rearrange("(it p) o -> p it o", p=128), waits=w)
            kb.wait(TEN, dtok[0], dtok[1])
            kb.wait(TEN, ck_v, v_gt)
            tk = None
            for tt in range(8):
                bank = (PA, PB, PC, PD, PE1, PF, PT1f, PT2f)[tt]
                if oc > 0:
                    kb.wait(TEN, ck_v, st["ev_v"].get(goc * 8 + tt - 8))
                else:
                    kb.wait(TEN, ck_v, v_gt)
                for it in range(nI):
                    tk = mm(bank[:, 0:256],
                            XBC[:, it * CSTR + tt * 128: it * CSTR + tt * 128 + 128],
                            wb[:, it * 256: it * 256 + 256],
                            start=(it == 0), stop=(it == nI - 1),
                            inc=ck_t if it == nI - 1 else None)
                dst = YTM[:, tt * 2048 + oc * 256: tt * 2048 + oc * 256 + 256]
                if accum_mode == "coef0":
                    st["ev_v"][goc * 8 + tt] = vop(
                        lambda e, dst=dst, bank=bank, tt=tt, e_i=e_i: e.tensor_scalar(
                            dst, bank[:, 0:256],
                            COEF[:, tt * 8 + e_i: tt * 8 + e_i + 1], None, OP.mult),
                        waits=[(ck_t, tk)])
                elif accum_mode == "coef":
                    st["ev_v"][goc * 8 + tt] = vop(
                        lambda e, dst=dst, bank=bank, tt=tt, e_i=e_i: e.scalar_tensor_tensor(
                            dst, bank[:, 0:256],
                            COEF[:, tt * 8 + e_i: tt * 8 + e_i + 1], dst,
                            OP.mult, OP.add), waits=[(ck_t, tk)])
                else:
                    st["ev_v"][goc * 8 + tt] = vop(
                        lambda e, dst=dst, bank=bank: e.tensor_tensor(
                            dst, dst, bank[:, 0:256], OP.add), waits=[(ck_t, tk)])
            st["y_mm"][goc] = tk

    def moe():
        moe_gate()
        if MOESTAGE <= 1:
            barrier()
            return
        st = {"git": 0, "goc": 0, "hg_mm": {}, "gt_v": {}, "y_mm": {}, "ev_v": {}}
        for e_i in range(E_MOE if MOESTAGE > 2 else 1):
            kb.wait(TEN, ck_v, ck_v.n)
            moe_mlp(e_i, w1[e_i], w3[e_i], w2[e_i], 0, IT_MOE,
                    "coef0" if e_i == 0 else "coef", st)
        if MOESTAGE <= 3:
            barrier()
            return
        for half in range(2):
            kb.wait(TEN, ck_v, ck_v.n)
            moe_mlp(0, sw1, sw3, sw2, half * IT_MOE, IT_MOE, "add", st)
        barrier()
        trp = {}
        for tt in range(8):
            for wv in range(2):
                pt = PT1 if wv == 0 else PT2
                w = [(ck_v, trp.get((tt - 1, wv)))]
                tk = None
                for i in range(8):
                    f = wv * 8 + i
                    tk = transp(pt[:, i * 128: i * 128 + 128],
                                YTM[:, tt * 2048 + f * 128: tt * 2048 + f * 128 + 128],
                                inc=ck_t if i == 7 else None, waits=w if i == 0 else ())
                last = None
                for i in range(8):
                    f = wv * 8 + i
                    last = vop(lambda e, pt=pt, i=i, f=f, tt=tt: e.tensor_add(
                        RES[:, f * T + tt * 128: f * T + tt * 128 + 128],
                        RES[:, f * T + tt * 128: f * T + tt * 128 + 128],
                        pt[:, i * 128: i * 128 + 128]), waits=[(ck_t, tk)])
                trp[(tt, wv)] = last
        barrier()

    def attn_qkv():
        blk = FT * 128
        mm_done, ev_done, st_done = {}, {}, {}
        for oi in range(48):
            bsl = WS[:, (oi % 2) * blk: (oi % 2) * blk + blk]
            w = [(ck_t, mm_done.get(oi - 2))] if oi >= 2 else []
            dtok = wdma(bsl.rearrange("p (kt o) -> p kt o", kt=FT),
                        WqkvT[:, oi * 128: oi * 128 + 128]
                        .rearrange("(kt p) o -> p kt o", p=128), waits=w)
            h, wq = oi // 3, oi % 3
            tk = None
            for th in range(2):
                bank = (PA, PB, PC, PD)[(oi * 2 + th) % 4]
                kb.wait(TEN, dtok[0], dtok[1])
                if oi * 2 + th >= 4:
                    kb.wait(TEN, ck_v, ev_done.get(oi * 2 + th - 4))
                for kt in range(FT):
                    tk = mm(bank[:, 0:512], bsl[:, kt * 128: kt * 128 + 128],
                            UB[:, kt * T + th * 512: kt * T + th * 512 + 512],
                            start=(kt == 0), stop=(kt == FT - 1),
                            inc=ck_t if kt == FT - 1 else None)
                wl = [(ck_t, tk)]
                if st_done.get(oi - 2) is not None:
                    wl.append(st_done[oi - 2])
                ev_done[oi * 2 + th] = vop(
                    lambda e, oi=oi, th=th, bank=bank: e.tensor_copy(
                        BOU[:, (oi % 2) * T + th * 512: (oi % 2) * T + th * 512 + 512],
                        bank[:, 0:512]), waits=wl)
            mm_done[oi] = tk
            dst = qdram[h] if wq == 0 else kv_src[h, wq - 1]
            st_done[oi] = adma(dst, BOU[:, (oi % 2) * T: (oi % 2) * T + T],
                               waits=[(ck_v, ev_done[oi * 2 + 1])])
        barrier()
        kb.op(GPS, lambda e: e.collective_compute(
            "AllGather", OP.bypass, replica_groups=PAIR_GROUPS,
            ins=[kv_src.ap()], outs=[kv_dst.ap()]),
            inc=ck_cc, waits=[(c, c.n) for c in ad_clks] + [(ck_g, ck_g.n)])
        barrier()

    def attn_core():
        hload, last_mm = {}, {}
        attn_pt, attn_ex = {}, {}
        v_ctx_ev = {}

        def load_head(h):
            b = h % 2
            w = [(ck_t, last_mm.get(h - 2))]
            toks = [adma(QH[:, b * T: b * T + T], qdram[h], waits=w)]
            for sl in range(2):
                toks.append(adma(KH[:, b * 2048 + sl * T: b * 2048 + sl * T + T],
                                 kv_dst[sl, h, 0]))
                toks.append(adma(VH[:, b * 2048 + sl * T: b * 2048 + sl * T + T],
                                 kv_dst[sl, h, 1]))
            hload[h] = toks

        load_head(0)
        load_head(1)
        for h in range(NH):
            b = h % 2
            for tok in hload[h]:
                kb.wait(TEN, tok[0], tok[1])
            for wv in range(2):
                pt = PT1 if wv == 0 else PT2
                w = [(ck_v, attn_pt.get(wv))]
                tk = None
                for i in range(8):
                    j = wv * 8 + i
                    tk = transp(pt[:, i * 128: i * 128 + 128],
                                VH[:, b * 2048 + j * 128: b * 2048 + j * 128 + 128],
                                inc=ck_t if i == 7 else None, waits=w if i == 0 else ())
                attn_pt[wv] = vop(lambda e, pt=pt, wv=wv: e.tensor_copy(
                    VTM[:, wv * 1024: wv * 1024 + 1024], pt[:]), waits=[(ck_t, tk)])
            kb.wait(TEN, ck_v, v_ctx_ev.get(h - 1))
            t0 = mm(PE1[:, 0:512], KH[:, b * 2048: b * 2048 + 128],
                    QH[:, b * T: b * T + 512], True, True, inc=ck_t)
            vop(lambda e: e.reduce_max(MXB[:, 0:1], PE1[:, 0:512], axis=AX.X),
                waits=[(ck_t, t0)])
            v_m1 = ck_v.n
            tmx = transp(PT1[0:1, 0:128], MXB[:, 0:1], inc=ck_t,
                         waits=[(ck_v, attn_pt[0]), (ck_v, v_m1)])
            vop(lambda e: e.reduce_max(MXB[0:1, 1:2], PT1[0:1, 0:128], axis=AX.X),
                waits=[(ck_t, tmx)])
            v_m2 = ck_v.n
            tcc = mm(PF[:, 0:1], onesrb[:], MXB[0:1, 1:2], True, True, inc=ck_t,
                     waits=[(ck_v, v_m2)])
            v_cc = vop(lambda e: e.tensor_scalar(CC[:], PF[:, 0:1], -RSQ, None,
                                                 OP.mult), waits=[(ck_t, tcc)])
            l_mm, l_ev = {}, {}
            for ih in range(2):
                for jt in range(16):
                    gjt = ih * 16 + jt
                    sb = PA if jt % 2 == 0 else PB
                    kb.wait(TEN, ck_s, attn_ex.get(gjt - 2))
                    if gjt == 0:
                        kb.wait(TEN, ck_v, v_cc)
                    ts = mm(sb[:, 0:512],
                            KH[:, b * 2048 + jt * 128: b * 2048 + jt * 128 + 128],
                            QH[:, b * T + ih * 512: b * T + ih * 512 + 512], True, True,
                            inc=ck_t)
                    wl = [(ck_t, ts)]
                    if ih == 1:
                        wl.append((ck_t, l_mm.get(0)))
                    attn_ex[gjt] = sop(
                        lambda e, sb=sb, jt=jt: e.activation(
                            UB[:, jt * 512: jt * 512 + 512], sb[:, 0:512], AF.Exp,
                            bias=CC[:, 0:1], scale=RSQ), waits=wl)
                    mm(PC[:, 0:512] if ih == 0 else PD[:, 0:512],
                       VTM[:, jt * 128: jt * 128 + 128],
                       UB[:, jt * 512: jt * 512 + 512],
                       start=(jt == 0), stop=(jt == 15),
                       inc=ck_t if jt == 15 else None,
                       waits=[(ck_s, attn_ex[gjt])])
                t_ctx = ck_t.n
                tl = None
                for jt in range(16):
                    if jt == 0:
                        kb.wait(TEN, ck_v, l_ev.get(ih - 1))
                    tl = mm(PE1[0:1, 0:512], onesc[:], UB[:, jt * 512: jt * 512 + 512],
                            start=(jt == 0), stop=(jt == 15),
                            inc=ck_t if jt == 15 else None)
                l_mm[ih] = tl
                l_ev[ih] = vop(lambda e, ih=ih: e.tensor_copy(
                    LROW[0:1, ih * 512: ih * 512 + 512],
                    PE1[0:1, 0:512]), waits=[(ck_t, tl)])
                if ih == 0:
                    t_ctx0 = t_ctx
                else:
                    t_ctx1 = t_ctx
            vop(lambda e: e.reciprocal(LROW[:], LROW[:]))
            v_li = ck_v.n
            li_ev = {}
            for ih in range(2):
                tb = mm(PF[:, 0:512], onesf[0:1, 0:128],
                        LROW[0:1, ih * 512: ih * 512 + 512], True, True, inc=ck_t,
                        waits=[(ck_v, v_li), (ck_v, li_ev.get(ih - 1))])
                li_ev[ih] = vop(lambda e, ih=ih: e.tensor_copy(
                    LINV[:, ih * 512: ih * 512 + 512],
                    PF[:, 0:512]), waits=[(ck_t, tb)])
            v_lb = ck_v.n
            last = None
            for ih in range(2):
                ctxb = PC if ih == 0 else PD
                last = vop(lambda e, h=h, ih=ih, ctxb=ctxb: e.tensor_mul(
                    YTM[:, h * T + ih * 512: h * T + ih * 512 + 512],
                    ctxb[:, 0:512], LINV[:, ih * 512: ih * 512 + 512]),
                    waits=[(ck_t, t_ctx0 if ih == 0 else t_ctx1), (ck_v, v_lb)])
            v_ctx_ev[h] = last
            last_mm[h] = ck_t.n
            adma(ctx_src[h], YTM[:, h * T: h * T + T], waits=[(ck_v, last)])
            if h + 2 < NH:
                load_head(h + 2)
        barrier()
        kb.op(GPS, lambda e: e.collective_compute(
            "AllGather", OP.bypass,
            replica_groups=[[0, 2, 4, 6], [1, 3, 5, 7]],
            ins=[ctx_src.ap()], outs=[ctx_dst.ap()]),
            inc=ck_cc, waits=[(c, c.n) for c in ad_clks] + [(ck_g, ck_g.n)])
        barrier()
        sel_v = {}
        for np_ in range(NH):
            toks = []
            for j in range(4):
                toks.append(adma(XBC[:, j * 1024: j * 1024 + 1024],
                                 ctx_dst[np_ // 4, 4 * (np_ % 4) + j],
                                 waits=[(ck_v, sel_v.get(np_ - 1))]))
            for th in range(2):
                last = vop(lambda e, np_=np_, th=th: e.tensor_scalar(
                    YTM[:, np_ * T + th * 512: np_ * T + th * 512 + 512],
                    XBC[:, th * 512: th * 512 + 512],
                    bsel_t[:, 0:1], None, OP.mult), waits=toks)
                for j in range(1, 4):
                    last = vop(lambda e, np_=np_, th=th, j=j: e.scalar_tensor_tensor(
                        YTM[:, np_ * T + th * 512: np_ * T + th * 512 + 512],
                        XBC[:, j * 1024 + th * 512: j * 1024 + th * 512 + 512],
                        bsel_t[:, j: j + 1],
                        YTM[:, np_ * T + th * 512: np_ * T + th * 512 + 512],
                        OP.mult, OP.add))
            sel_v[np_] = ck_v.n
        barrier()

    def attn_proj():
        blk = FT * 128
        mm_done, epi_done = {}, {}
        for oi in range(FT):
            bsl = WS[:, (oi % 2) * blk: (oi % 2) * blk + blk]
            w = [(ck_t, mm_done.get(oi - 2))] if oi >= 2 else []
            dtok = wdma(bsl.rearrange("p (kt o) -> p kt o", kt=FT),
                        WprojT[:, oi * 128: oi * 128 + 128]
                        .rearrange("(kt p) o -> p kt o", p=128), waits=w)
            tk = None
            for th in range(2):
                bank = (PA, PB, PC, PD)[(oi * 2 + th) % 4]
                kb.wait(TEN, dtok[0], dtok[1])
                if oi * 2 + th >= 4:
                    kb.wait(TEN, ck_v, epi_done.get(oi * 2 + th - 4))
                for kt in range(FT):
                    tk = mm(bank[:, 0:512], bsl[:, kt * 128: kt * 128 + 128],
                            YTM[:, kt * T + th * 512: kt * T + th * 512 + 512],
                            start=(kt == 0), stop=(kt == FT - 1),
                            inc=ck_t if kt == FT - 1 else None)
                epi_done[oi * 2 + th] = vop(
                    lambda e, oi=oi, th=th, bank=bank: e.scalar_tensor_tensor(
                        RES[:, oi * T + th * 512: oi * T + th * 512 + 512],
                        bank[:, 0:512], bprj_t[:, oi: oi + 1],
                        RES[:, oi * T + th * 512: oi * T + th * 512 + 512],
                        OP.add, OP.add), waits=[(ck_t, tk)])
            mm_done[oi] = tk
        barrier()

    def attention():
        attn_qkv()
        attn_core()
        attn_proj()

    def program():
        ROT["wd"] = 0
        ROT["ad"] = 0
        setup()
        if NSUB >= 1:
            layernorm()
            mamba()
        if NSUB >= 2:
            layernorm()
            moe()
        if NSUB >= 3:
            layernorm()
            attention()
        if NSUB >= 4:
            layernorm()
            moe()
        barrier()
        wdma(out_fm.rearrange("(f p) t -> p f t", p=128),
             RES[:].rearrange("p (f t) -> p f t", f=FT))
        for eng in (TEN, VEC, SCA, GPS, SYN):
            wait_all_dma(eng)

    with nc.Block() as block:
        @block.tensor
        def _(e):
            kb.begin(TEN, e)
            program()

        @block.vector
        def _(e):
            kb.begin(VEC, e)
            program()

        @block.scalar
        def _(e):
            kb.begin(SCA, e)
            program()

        @block.gpsimd
        def _(e):
            kb.begin(GPS, e)
            program()

        @block.sync
        def _(e):
            kb.begin(SYN, e)
            program()

    es.close()
    return nc


_NC_CACHE = {}


def _prep_in_maps(inputs):
    x = np.asarray(inputs["L0_attn_a_hidden_states"], dtype=np.float32)
    shared = {
        "W_in": np.ascontiguousarray(inputs["L0_attn_a_W_in"], np.float32),
        "b_in": np.ascontiguousarray(inputs["L0_attn_a_b_in"], np.float32),
        "mup_vec": np.ascontiguousarray(inputs["L0_attn_a_mup_vector"], np.float32),
        "conv_w": np.ascontiguousarray(inputs["L0_attn_a_conv_w"], np.float32),
        "conv_b": np.ascontiguousarray(inputs["L0_attn_a_conv_b"], np.float32),
        "A_log": np.ascontiguousarray(inputs["L0_attn_a_A_log"], np.float32),
        "D_par": np.ascontiguousarray(inputs["L0_attn_a_D_param"], np.float32),
        "dt_bias": np.ascontiguousarray(inputs["L0_attn_a_dt_bias"], np.float32),
        "norm_w": np.ascontiguousarray(inputs["L0_attn_a_norm_w"], np.float32),
        "W_out": np.ascontiguousarray(inputs["L0_attn_a_W_out"], np.float32),
        "b_out": np.ascontiguousarray(inputs["L0_attn_a_b_out"], np.float32),
        "WqkvT": np.ascontiguousarray(np.asarray(inputs["L0_attn_b_Wqkv"], np.float32).T),
        "WprojT": np.ascontiguousarray(np.asarray(inputs["L0_attn_b_Wproj"], np.float32).T),
        "bproj": np.ascontiguousarray(inputs["L0_attn_b_bproj"], np.float32),
        "gate_wT": np.ascontiguousarray(np.asarray(inputs["L0_ffn_gate_w"], np.float32).T),
        "w1": np.ascontiguousarray(inputs["L0_ffn_w1"], np.float32),
        "w3": np.ascontiguousarray(inputs["L0_ffn_w3"], np.float32),
        "w2": np.ascontiguousarray(inputs["L0_ffn_w2"], np.float32),
        "sw1": np.ascontiguousarray(inputs["L0_ffn_sw1"], np.float32),
        "sw3": np.ascontiguousarray(inputs["L0_ffn_sw3"], np.float32),
        "sw2": np.ascontiguousarray(inputs["L0_ffn_sw2"], np.float32),
    }
    def packo(v, width):
        v = np.asarray(v, np.float32).ravel()
        out = np.zeros((128, width), np.float32)
        n = v.size // 128
        out[:, :n] = v[: n * 128].reshape(n, 128).T
        rem = v.size - n * 128
        if rem:
            out[:rem, n] = v[n * 128:]
        return out

    shared["bin_h"] = packo(inputs["L0_attn_a_b_in"], 37)
    shared["mup_h"] = packo(inputs["L0_attn_a_mup_vector"], 37)
    cw = np.asarray(inputs["L0_attn_a_conv_w"], np.float32).reshape(20, 128, 4)
    shared["convw_h"] = np.ascontiguousarray(
        cw.transpose(1, 0, 2).reshape(128, 80))
    shared["convb_h"] = packo(inputs["L0_attn_a_conv_b"], 20)
    shared["normw_h"] = packo(inputs["L0_attn_a_norm_w"], 16)
    shared["bout_h"] = packo(inputs["L0_attn_a_b_out"], 16)
    shared["bprj_h"] = packo(inputs["L0_attn_b_bproj"], 16)
    shared["a32_h"] = np.stack([np.asarray(inputs["L0_attn_a_A_log"], np.float32),
                                np.asarray(inputs["L0_attn_a_D_param"], np.float32),
                                np.asarray(inputs["L0_attn_a_dt_bias"], np.float32)],
                               axis=1)
    gwt_full = np.asarray(inputs["L0_ffn_gate_w"], np.float32).T
    shared["gwt_h"] = np.ascontiguousarray(
        gwt_full.reshape(16, 128, 8).transpose(1, 0, 2).reshape(128, 128))
    in_maps = []
    for c in range(8):
        b, hh = c // 2, c % 2
        m = dict(shared)
        m["x_fm"] = np.ascontiguousarray(x[b, hh * T:(hh + 1) * T, :].T)
        xb = np.zeros((D, 4), np.float32)
        if hh == 1:
            xb[:, 0:3] = x[b, T - 3: T, :].T
            m["bflag"] = np.ones((128, 1), np.float32)
        else:
            m["bflag"] = np.zeros((128, 1), np.float32)
        m["xbnd_t"] = np.ascontiguousarray(
            xb.reshape(16, 128, 4).transpose(1, 0, 2).reshape(128, 64))
        bs = np.zeros((128, 4), np.float32)
        bs[:, b] = 1.0
        m["bsel"] = bs
        in_maps.append(m)
    return in_maps


def kernel(**inputs):
    if "prog" not in _NC_CACHE:
        _NC_CACHE["prog"] = build_program()
    nc = _NC_CACHE["prog"]
    in_maps = _prep_in_maps(inputs)
    res = run_bass_kernel_spmd(nc, in_maps, core_ids=list(range(8)))
    out = np.empty((B, L, D), np.float32)
    for c in range(8):
        b, hh = c // 2, c % 2
        out[b, hh * T:(hh + 1) * T, :] = np.asarray(res.results[c]["out_fm"]).T
    return out
